# version 16
# baseline (speedup 1.0000x reference)
import numpy as np
import ml_dtypes
import concourse.bass as bass
import concourse.mybir as mybir
from concourse.bass_utils import run_bass_kernel_spmd

F32 = mybir.dt.float32
BF16 = mybir.dt.bfloat16
OP = mybir.AluOpType
AF = mybir.ActivationFunctionType
AX = mybir.AxisListType

B, S, D = 16, 2048, 1024
NCORE = 8
SEQ = 2
NTCH = S // 128
H, HD = 16, 64
DQ = DKV = 512
E, R = 8, 64
DI, DS = 2048, 16
TCH = 512
NMCH = S // TCH
NDC = DI // 128
SCALE_ATT = float(1.0 / np.sqrt(16 + DQ + DKV))
EPS = 1e-5

ENGS = ("pe", "dve", "act", "pool", "sp")
DMA_ROLES = ("xin0", "xin1", "wa0", "wa1", "wf0", "wf1", "wm0", "wm1", "wm2", "wm3", "wmc", "wmb", "cst", "cs0", "cs1", "bc0", "bc1", "bcw", "sc", "sc2", "out")


class Sch:
    def __init__(self, nc):
        self.nc = nc
        self.sem = {e: nc.alloc_semaphore(f"sem_{e}") for e in ENGS}
        self.dsem = {r: nc.alloc_semaphore(f"dma_{r}") for r in DMA_ROLES}
        self.ops = {e: [] for e in ENGS}
        self.icnt = {e: 0 for e in ENGS}
        self.incs = {e: {} for e in ENGS}
        self.mlist = {e: [] for e in ENGS}
        self.mval = {e: 0 for e in ENGS}
        self.dissued = {r: 0 for r in DMA_ROLES}
        self.waited = {e: {} for e in ENGS}
        self.writer = {}
        self.readers = {}

    def _wait(self, eng, semobj, val, key):
        if self.waited[eng].get(key, 0) >= val:
            return
        self.ops[eng].append(("wait", semobj, val))
        self.waited[eng][key] = val

    def _milestone_for(self, e, idx):
        for cov, val in self.mlist[e]:
            if cov >= idx:
                return val
        last = self.icnt[e]
        assert last >= idx, f"milestone for {e}@{idx} but only {last} instrs"
        self.mval[e] += 1
        if e in ("dve", "pool"):
            for _ in range(3):
                self.ops[e].append(("call", lambda E: E.engine_nop()))
            self.icnt[e] = last + 3
            self.incs[e].setdefault(last + 2, []).append((self.sem[e], 1))
            self.mlist[e].append((last + 3, self.mval[e]))
        else:
            self.incs[e].setdefault(last - 1, []).append((self.sem[e], 1))
            self.mlist[e].append((last, self.mval[e]))
        return self.mval[e]

    def _need(self, eng, prod):
        kind, who, idx = prod
        if kind == "eng":
            if who == eng:
                return
            self._wait(eng, self.sem[who], self._milestone_for(who, idx), ("e", who))
        else:
            self._wait(eng, self.dsem[who], 16 * self.dissued[who], ("d", who))

    def _pre(self, eng, reads, writes):
        for b in reads:
            w = self.writer.get(b)
            if w is not None:
                self._need(eng, w)
        for b in writes:
            w = self.writer.get(b)
            if w is not None:
                self._need(eng, w)
            for r in self.readers.get(b, ()):
                self._need(eng, r)

    GAP_RAW = 4
    GAP_WAR = 2

    def _spacer(self, eng, n):
        for _ in range(n):
            self.ops[eng].append(("call", lambda E: E.engine_nop()))
            self.icnt[eng] += 1

    def op(self, eng, fn, reads=(), writes=()):
        self._pre(eng, reads, writes)
        if eng in ("dve", "pool"):
            need = 0
            for b in reads:
                w = self.writer.get(b)
                if w is not None and w[0] == "eng" and w[1] == eng:
                    need = max(need, self.GAP_RAW - (self.icnt[eng] - w[2]))
            for b in writes:
                w = self.writer.get(b)
                if w is not None and w[0] == "eng" and w[1] == eng:
                    need = max(need, self.GAP_WAR - (self.icnt[eng] - w[2]))
                for r in self.readers.get(b, ()):
                    if r[0] == "eng" and r[1] == eng:
                        need = max(need, self.GAP_WAR - (self.icnt[eng] - r[2]))
            if need > 0:
                self._spacer(eng, need)
        self.ops[eng].append(("call", fn))
        self.icnt[eng] += 1
        idx = self.icnt[eng]
        for b in reads:
            self.readers.setdefault(b, []).append(("eng", eng, idx))
        for b in writes:
            self.writer[b] = ("eng", eng, idx)
            self.readers[b] = []

    def dma(self, dst, src, role, reads=(), writes=(), queue="sp"):
        eng = queue
        self._pre(eng, reads, writes)
        self.ops[eng].append(("call", lambda E, dst=dst, src=src, s=self.dsem[role]:
                              E.dma_start(out=dst, in_=src).then_inc(s, 16)))
        self.icnt[eng] += 1
        self.dissued[role] += 1
        n = self.dissued[role]
        for b in reads:
            self.readers.setdefault(b, []).append(("dma", role, n))
        for b in writes:
            self.writer[b] = ("dma", role, n)
            self.readers[b] = []

    def alias(self, newkey, *oldkeys):
        rd = list(self.readers.get(newkey, ()))
        wr = self.writer.get(newkey)
        for ok in oldkeys:
            w = self.writer.get(ok)
            if w is not None:
                rd.append(w)
            rd.extend(self.readers.get(ok, ()))
        if wr is not None:
            rd.append(wr)
        self.readers[newkey] = rd
        self.writer.pop(newkey, None)

    def finalize(self):
        nc = self.nc
        self.ops["sp"].append(("wait", self.dsem["out"], 16 * self.dissued["out"]))
        with nc.Block() as blk:
            def mkbody(e):
                def body(E):
                    i = 0
                    for item in self.ops[e]:
                        if item[0] == "wait":
                            E.wait_ge(item[1], item[2])
                        else:
                            ins = item[1](E)
                            for so, k in self.incs[e].get(i, ()):
                                ins.then_inc(so, k)
                            i += 1
                return body
            blk.tensor(mkbody("pe"))
            blk.vector(mkbody("dve"))
            blk.scalar(mkbody("act"))
            blk.gpsimd(mkbody("pool"))
            blk.sync(mkbody("sp"))



def _colform(b):
    return np.ascontiguousarray(np.asarray(b, np.float32).reshape(-1, 128).T)


ROWOFF = {}
_off = 0
for _n, _w in (("buq", D), ("bqr", D), ("bkr", D), ("buv", D), ("bo", D),
               ("bg", 8), ("bc", 32), ("outb", D)):
    ROWOFF[_n] = (_off, _w)
    _off += _w
NROWS = _off


def host_prepare(inp):
    g = {}
    for k in ("Wd_q_w", "Wu_q_w", "Wq_r_w", "Wk_r_w", "Wd_kv_w", "Wu_k_w",
              "Wu_v_w", "Wo_w"):
        g[k] = np.ascontiguousarray(np.asarray(inp[f"L0_attn_a_{k}"], np.float32))
    g["bq_col"] = _colform(inp["L0_attn_a_Wd_q_b"])
    g["bkv_col"] = _colform(inp["L0_attn_a_Wd_kv_b"])
    g["buk_col"] = _colform(inp["L0_attn_a_Wu_k_b"])
    rows = [np.asarray(inp[f"L0_attn_a_{n}"], np.float32).reshape(-1) for n in
            ("Wu_q_b", "Wq_r_b", "Wk_r_b", "Wu_v_b", "Wo_b")]
    rows += [np.asarray(inp["L0_ffn_bg"], np.float32).reshape(-1),
             np.concatenate([np.asarray(inp["L0_attn_b_B_b"], np.float32),
                             np.asarray(inp["L0_attn_b_C_b"], np.float32)]),
             np.asarray(inp["L0_attn_b_out_b"], np.float32).reshape(-1)]
    g["rows_f32"] = np.ascontiguousarray(np.concatenate(rows).reshape(1, -1))
    g["WgT"] = np.ascontiguousarray(np.asarray(inp["L0_ffn_Wg"], np.float32).T)
    g["W1r"] = np.ascontiguousarray(
        np.asarray(inp["L0_ffn_W1"], np.float32).transpose(2, 0, 1).reshape(D, E * R))
    g["W2s"] = np.ascontiguousarray(
        np.asarray(inp["L0_ffn_W2"], np.float32).transpose(0, 2, 1).reshape(E * R, D))
    g["in_w"] = np.ascontiguousarray(np.asarray(inp["L0_attn_b_in_w"], np.float32))
    g["dt_w"] = np.ascontiguousarray(np.asarray(inp["L0_attn_b_dt_w"], np.float32))
    g["out_w"] = np.ascontiguousarray(np.asarray(inp["L0_attn_b_out_w"], np.float32))
    bcw = np.concatenate([np.asarray(inp["L0_attn_b_B_w"], np.float32),
                          np.asarray(inp["L0_attn_b_C_w"], np.float32)], axis=1)
    g["BC_w"] = np.ascontiguousarray(
        bcw.reshape(16, 128, 32).transpose(1, 0, 2).reshape(128, 512))
    g["in_b_col"] = _colform(inp["L0_attn_b_in_b"])
    cw = np.asarray(inp["L0_attn_b_conv_w"], np.float32)[:, 0, :].T
    g["conv_col"] = np.ascontiguousarray(
        cw.reshape(NDC, 128, 4).transpose(1, 0, 2).reshape(128, NDC * 4))
    g["convb_col"] = _colform(inp["L0_attn_b_conv_b"])
    g["dtb_col"] = _colform(inp["L0_attn_b_dt_b"])
    g["Dp_col"] = _colform(inp["L0_attn_b_Dp"])
    Am = np.asarray(inp["L0_attn_b_A"], np.float32)
    g["A_col"] = np.ascontiguousarray(
        Am.reshape(NDC, 128, DS).transpose(1, 0, 2).reshape(128, NDC * DS))
    pos = np.arange(S, dtype=np.float64)[:, None]
    ids = np.arange(D // 2, dtype=np.float64)
    theta = (10000.0 ** (-2.0 * ids)) / (D // 2)
    r = pos * theta[None, :]
    g["cos_t"] = np.cos(r).astype(ml_dtypes.bfloat16)
    g["sin_t"] = np.sin(r).astype(ml_dtypes.bfloat16)
    g["ident_bf"] = np.eye(128).astype(ml_dtypes.bfloat16)
    g["ones_bf"] = np.ones((128, 128), dtype=ml_dtypes.bfloat16)
    g["epsvec"] = np.full((128, 1), EPS, dtype=np.float32)
    g["onevec"] = np.full((128, 1), 1.0, dtype=np.float32)
    return g


TAP = [None]
LAST_SCH = None


def build_program(n_sub=4):
    nc = bass.Bass("TRN2", target_bir_lowering=False)
    sch = Sch(nc)

    dx = nc.dram_tensor("x", [SEQ * S, D], F32, kind="ExternalInput").ap()
    dout = nc.dram_tensor("out", [SEQ * S, D], F32, kind="ExternalOutput").ap()
    dw = {}
    wshapes = {
        "Wd_q_w": (D, DQ), "Wu_q_w": (DQ, D), "Wq_r_w": (DQ, D), "Wk_r_w": (D, D),
        "Wd_kv_w": (D, DKV), "Wu_k_w": (DKV, D), "Wu_v_w": (DKV, D), "Wo_w": (D, D),
        "bq_col": (128, 4), "bkv_col": (128, 4), "buk_col": (128, 8),
        "rows_f32": (1, NROWS), "WgT": (D, 8), "W1r": (D, E * R), "W2s": (E * R, D),
        "in_w": (D, 2 * DI), "dt_w": (DI, DI), "out_w": (DI, D), "BC_w": (128, 512),
        "in_b_col": (128, 32), "conv_col": (128, NDC * 4), "convb_col": (128, NDC),
        "dtb_col": (128, NDC), "Dp_col": (128, NDC), "A_col": (128, NDC * DS),
        "epsvec": (128, 1), "onevec": (128, 1),
    }
    for name, shape in wshapes.items():
        dw[name] = nc.dram_tensor(name, list(shape), F32, kind="ExternalInput").ap()
    for name in ("cos_t", "sin_t"):
        dw[name] = nc.dram_tensor(name, [S, D // 2], BF16, kind="ExternalInput").ap()
    for name in ("ident_bf", "ones_bf"):
        dw[name] = nc.dram_tensor(name, [128, 128], BF16, kind="ExternalInput").ap()

    scr = {k: nc.dram_tensor("scr_" + k, list(sh), BF16, kind="Internal").ap()
           for k, sh in (("in_w", (D, 2 * DI)), ("dt_w", (DI, DI)),
                         ("out_w", (DI, D)), ("BC_w", (128, 512)))}
    scr_bc = nc.dram_tensor("scr_bc", [32, TCH], BF16, kind="Internal").ap()
    scr_mh = nc.dram_tensor("scr_mh", [1, 8], F32, kind="Internal").ap()
    scr_res = nc.dram_tensor("scr_res", [S, D], BF16, kind="Internal").ap()

    AB = 16512
    CAP = 229376
    cur = [AB]

    def place(name, shape, dtype, at=None):
        nbytes = int(np.prod(shape[1:])) * (2 if dtype == BF16 else 4)
        if at is None:
            at = cur[0]
            cur[0] = at + ((nbytes + 31) // 32) * 32
        assert at + nbytes <= CAP, f"{name}: {at}+{nbytes} > {CAP}"
        return nc.alloc_sbuf_tensor_at(name, list(shape), dtype, offset=at).ap()

    res = place("res", (128, SEQ * NTCH * D), BF16)
    ident = place("ident", (128, 128), BF16)
    ones_bf = place("ones", (128, 128), BF16)
    cA = place("cA", (128, NDC * DS), F32)
    cconv = place("cconv", (128, NDC * 4), F32)
    cconvb = place("cconvb", (128, NDC), F32)
    cdtb = place("cdtb", (128, NDC), F32)
    cDp = place("cDp", (128, NDC), F32)
    cinb = place("cinb", (128, 32), F32)
    cbq = place("cbq", (128, 4), F32)
    cbkv = place("cbkv", (128, 4), F32)
    cbuk = place("cbuk", (128, 8), F32)
    ceps = place("ceps", (128, 1), F32)
    cone = place("cone", (128, 1), F32)
    rows_all = place("rows_all", (1, NROWS), BF16)
    mv = place("mv", (128, 2 * NTCH), F32)
    rstd = place("rstd", (128, NTCH), F32)
    nrm = place("nrm", (128, 64), F32)
    mhat = place("mhat", (128, 1), F32)
    gm1n = place("gm1n", (128, 1), F32)
    glg = place("glg", (128, 8), F32)
    gm1 = place("gm1", (128, 1), F32)
    gm2 = place("gm2", (128, 1), F32)
    gzz = place("gzz", (128, 1), F32)
    gzr = place("gzr", (128, 1), F32)
    gtA = place("gtA", (128, 8), F32)
    gtB = place("gtB", (128, 8), F32)
    mh_bf = place("mh_bf", (1, 1), BF16)
    lntile = place("lntile", (128, D), BF16)
    ropet = place("ropet", (128, D), BF16)
    csbuf = place("csbuf", (128, D), BF16)
    ctq_c = place("ctq_c", (128, DQ), BF16)
    rsb = place("rsb", (128, 256), F32)
    lntile2 = ropet
    FIXEND = cur[0]

    WST = cur[0]
    wstage = place("wstage", (128, 2048), F32, at=WST)
    xstage = place("xstage", (128, 2048), F32, at=WST)
    cur[0] = WST + 8192
    BIG = cur[0]
    assert CAP - BIG >= 112 * 1024, f"BIG={CAP - BIG}"

    QT = place("QT", (128, 8 * S), BF16, at=BIG)
    KT = place("KT", (128, 8 * S), BF16, at=BIG + 32768)
    xnT = place("xnT", (128, 8 * S), BF16, at=BIG + 65536)
    V65 = place("V65", (128, NTCH * H * 65), BF16, at=BIG + 65536)
    RW = BIG + ((65536 + NTCH * H * 65 * 2 + 31) // 32) * 32
    w_a = place("w_a", (128, 4096), BF16, at=RW)
    w_b = place("w_b", (128, 4096), BF16, at=RW + 8192)
    w_ab = place("w_ab", (128, 8192), BF16, at=RW)
    rs_bf = place("rs_bf", (128, 512), BF16, at=RW)
    rbb = place("rbb", (128, 512), BF16, at=RW + 1024)
    sqb = place("sqb", (128, 512), BF16, at=RW + 2048)
    rows_f32s = place("rows_f32s", (1, NROWS), F32, at=BIG + 65536)
    assert RW + 16384 <= CAP, f"attn overlay overflow: {RW + 16384 - CAP}"

    f_xnT = place("f_xnT", (128, 8 * SEQ * S), BF16, at=BIG)
    f_w1 = place("f_w1", (128, 8 * E * R), BF16, at=BIG + 65536)
    f_w2 = place("f_w2", (128, 4 * D), BF16, at=BIG + 73728)
    f_wg = place("f_wg", (128, 64), BF16, at=BIG + 81920)
    f_g = place("f_g", (128, SEQ * NTCH * E), F32, at=BIG + 82176)
    f_gh = place("f_gh", (128, E * R), BF16, at=BIG + 86272)
    f_ghT = place("f_ghT", (128, E * R), BF16, at=BIG + 87296)

    MB = BIG
    m_xnT = place("m_xnT", (128, 8 * TCH), BF16, at=MB)
    m_xz = place("m_xz", (128, NDC * (TCH + 3)), BF16, at=MB + 8192)
    m_dt = place("m_dt", (128, NDC * TCH), BF16, at=MB + 8192)
    m_xc = place("m_xc", (128, NDC * TCH), BF16, at=MB + 25088)
    m_sg = place("m_sg", (128, NDC * TCH), BF16, at=MB + 41472)
    m_v = place("m_v", (128, 4 * TCH), BF16, at=MB + 57856)
    m_ye = place("m_ye", (128, 4 * TCH), BF16, at=MB + 61952)
    m_yo = place("m_yo", (128, 4 * TCH), BF16, at=MB + 66048)
    m_bc = place("m_bc", (128, 4 * TCH), BF16, at=MB + 70144)
    m_dA = place("m_dA", (128, 4 * TCH), F32, at=MB + 96000)
    m_sp = place("m_sp", (128, 2 * TCH), F32, at=MB + 74240)
    m_h = place("m_h", (128, 4 * TCH), BF16, at=MB + 104192)
    m_dbx = place("m_dbx", (128, 4 * TCH), BF16, at=MB + 108288)
    m_carry = place("m_carry", (128, SEQ * NDC * DS), F32, at=MB + 82432)
    m_bmcm = place("m_bmcm", (128, 4 * 32), BF16, at=MB + 84480)
    m_bmT = place("m_bmT", (32, 512), BF16, at=MB + 84736)
    m_haloA = place("m_haloA", (128, SEQ * NDC * 3), BF16, at=MB + 85760)
    m_mini = place("m_mini", (128, 64), BF16, at=MB + 86144)
    m_wk = place("m_wk", (128, 4 * 1024), BF16, at=MB + 87424)
    m_bcw = place("m_bcw", (128, 512), BF16, at=MB + 86400)
    MEND = MB + 112384
    assert BIG + 112384 <= CAP, f"mamba overlay {BIG + 112384 - CAP}"

    pb = [nc.alloc_psum_tensor(f"pb{i}", [128, 512], F32).ap() for i in range(8)]
    pbh = [p.bitcast(BF16) for p in pb]

    for dst, name in ((ident, "ident_bf"), (ones_bf, "ones_bf"), (cA, "A_col"),
                      (cconv, "conv_col"), (cconvb, "convb_col"), (cdtb, "dtb_col"),
                      (cDp, "Dp_col"), (cinb, "in_b_col"), (cbq, "bq_col"),
                      (cbkv, "bkv_col"), (cbuk, "buk_col"), (ceps, "epsvec"),
                      (cone, "onevec")):
        sch.dma(dst, dw[name], "cst", writes=("consts",))
    sch.dma(rows_f32s, dw["rows_f32"], "cst", writes=("rows_f",))
    sch.op("dve", lambda v: v.tensor_copy(out=rows_all, in_=rows_f32s),
           reads=("rows_f",), writes=("consts",))

    def row(name, lo=0, hi=None):
        o, w = ROWOFF[name]
        if hi is None:
            hi = w
        return rows_all[:, o + lo:o + hi]

    for t in range(SEQ * NTCH):
        hh = t % 2
        sl = xstage[:, hh * 1024:(hh + 1) * 1024]
        sch.dma(sl, dx[t * 128:(t + 1) * 128, :], f"xin{hh}", writes=(f"xs{hh}",))
        sch.op("dve", lambda v, sl=sl, t=t: v.tensor_copy(
            out=res[:, t * D:(t + 1) * D], in_=sl),
            reads=(f"xs{hh}",), writes=(f"res{t}",))

    def cast_weights():
        for wname in ("in_w", "dt_w", "out_w", "BC_w"):
            sch.dma(scr[wname], dw[wname], "wmc", writes=(f"scr_{wname}",),
                    queue="pool")

    def ln_chunk(t, slot, outtile, okey):
        rt = res[:, t * D:(t + 1) * D]
        rk = f"res{t}"
        mcol = mv[:, slot * 2:slot * 2 + 1]
        vcol = mv[:, slot * 2 + 1:slot * 2 + 2]
        sch.op("dve", lambda v, rt=rt, mcol=mcol: v.tensor_reduce(
            out=mcol, in_=rt, axis=AX.X, op=OP.add),
            reads=(rk,), writes=(f"mv{slot}",))
        sch.op("act", lambda s_, rt=rt: s_.activation(
            out=csbuf, in_=rt, func=AF.Square),
            reads=(rk,), writes=("csb",))
        sch.op("dve", lambda v, vcol=vcol: v.tensor_reduce(
            out=vcol, in_=csbuf, axis=AX.X, op=OP.add),
            reads=("csb",), writes=(f"mv{slot}",))
        sch.op("dve", lambda v, mcol=mcol: v.tensor_scalar(
            out=mcol, in0=mcol, scalar1=1.0 / D, scalar2=None, op0=OP.mult),
            reads=(f"mv{slot}",), writes=(f"mv{slot}",))
        sch.op("dve", lambda v, vcol=vcol: v.tensor_scalar(
            out=vcol, in0=vcol, scalar1=1.0 / D, scalar2=None, op0=OP.mult),
            reads=(f"mv{slot}",), writes=(f"mv{slot}",))
        sch.op("dve", lambda v, mcol=mcol, vcol=vcol: v.scalar_tensor_tensor(
            out=vcol, in0=mcol, scalar=mcol, in1=vcol, op0=OP.mult,
            op1=OP.subtract),
            reads=(f"mv{slot}",), writes=(f"mv{slot}",))
        rs = rstd[:, slot:slot + 1]
        sch.op("act", lambda s_, vcol=vcol, rs=rs: s_.activation(
            out=rs, in_=vcol, func=AF.Ln, bias=ceps, scale=-1.0),
            reads=(f"mv{slot}", "consts"), writes=(f"rstd{slot}",))
        sch.op("act", lambda s_, rs=rs: s_.activation(
            out=rs, in_=rs, func=AF.Exp, scale=-0.5),
            reads=(f"rstd{slot}",), writes=(f"rstd{slot}",))
        sch.op("dve", lambda v, rt=rt, outtile=outtile, mcol=mcol, rs=rs:
               v.tensor_scalar(out=outtile, in0=rt, scalar1=mcol,
                               scalar2=rs, op0=OP.subtract, op1=OP.mult),
               reads=(rk, f"mv{slot}", f"rstd{slot}"), writes=(okey,))

    def transp8(intile, inkey, dstfn, dkey, bank=6, bank2=7):
        for j in range(8):
            bk = bank if j % 2 == 0 else bank2
            sl = pbh[bk][:, ((j // 2) % 2) * 128:((j // 2) % 2) * 128 + 128]
            sch.op("pe", lambda T, j=j, sl=sl, intile=intile: T.transpose(
                out=sl, in_=intile[:, j * 128:j * 128 + 128], identity=ident),
                reads=(inkey, "consts"), writes=(f"ps{bk}",))
            sch.op("dve", lambda v, j=j, sl=sl, dstfn=dstfn: v.tensor_copy(
                out=dstfn(j), in_=sl), reads=(f"ps{bk}",), writes=(dkey,))

    def load_w(dramap, dst, dstkey, role="wa"):
        K, M = dramap.shape
        for k in range(K // 128):
            for m0 in range(0, M, 1024):
                mw = min(1024, M - m0)
                half = ((k * M + m0) // 1024) % 2
                st = wstage[:, half * 1024:half * 1024 + mw]
                sk = f"wst{half}"
                sch.dma(st, dramap[k * 128:(k + 1) * 128, m0:m0 + mw],
                        role + str(half), writes=(sk,))
                sch.op("dve", lambda v, st=st, dst=dst, k=k, m0=m0, M=M, mw=mw:
                       v.tensor_copy(out=dst[:, k * M + m0:k * M + m0 + mw], in_=st),
                       reads=(sk,), writes=(dstkey,))

    def rope_psum(pbase):
        rk = (f"ps{pbase}", f"ps{pbase + 1}", "csb")
        for half in range(2):
            p = pb[pbase + half]
            c0 = csbuf[:, half * 256:half * 256 + 256]
            s0 = csbuf[:, 512 + half * 256:512 + half * 256 + 256]
            ev = ropet[:, half * 512:half * 512 + 512:2]
            od = ropet[:, half * 512 + 1:half * 512 + 512:2]
            pe_ = p[:, 0:512:2]
            po_ = p[:, 1:512:2]
            t1 = rsb[:, 0:256]
            sch.op("dve", lambda v, pe_=pe_, c0=c0, t1=t1: v.tensor_tensor(
                out=t1, in0=pe_, in1=c0, op=OP.mult), reads=rk, writes=("rt1",))
            sch.op("dve", lambda v, po_=po_, s0=s0, ev=ev: v.tensor_tensor(
                out=ev, in0=po_, in1=s0, op=OP.mult), reads=rk, writes=("ropet",))
            sch.op("dve", lambda v, ev=ev, t1=t1: v.scalar_tensor_tensor(
                out=ev, in0=ev, scalar=-1.0, in1=t1, op0=OP.mult, op1=OP.add),
                reads=("rt1", "ropet"), writes=("ropet",))
            sch.op("dve", lambda v, po_=po_, c0=c0, t1=t1: v.tensor_tensor(
                out=t1, in0=po_, in1=c0, op=OP.mult), reads=rk, writes=("rt1",))
            sch.op("dve", lambda v, pe_=pe_, s0=s0, od=od: v.tensor_tensor(
                out=od, in0=pe_, in1=s0, op=OP.mult), reads=rk, writes=("ropet",))
            sch.op("dve", lambda v, od=od, t1=t1: v.tensor_tensor(
                out=od, in0=od, in1=t1, op=OP.add),
                reads=("rt1", "ropet"), writes=("ropet",))

    def norm_bound():
        sch.op("dve", lambda v: v.memset(nrm, 0.0), writes=("nrm",))
        for src, srckey, col in ((QT, "QT", 0), (KT, "KT", 1)):
            for tb in range(4):
                for k in range(8):
                    sch.op("act", lambda s_, src=src, k=k, tb=tb: s_.activation(
                        out=sqb, in_=src[:, k * S + tb * 512:k * S + tb * 512 + 512],
                        func=AF.Square), reads=(srckey,), writes=("sqb",))
                    sch.op("pe", lambda T, k=k: T.matmul(
                        out=pb[6][0:1, :], lhsT=ones_bf[:, 0:1], rhs=sqb,
                        start=(k == 0), stop=(k == 7)),
                        reads=("sqb", "consts"), writes=("ps6",))
                sch.op("dve", lambda v, col=col: v.tensor_reduce(
                    out=nrm[0:1, 32 + col * 8:33 + col * 8], in_=pb[6][0:1, :],
                    axis=AX.X, op=OP.max), reads=("ps6",), writes=("nrm",))
                sch.op("dve", lambda v, col=col: v.tensor_tensor(
                    out=nrm[0:1, col * 8:col * 8 + 1], in0=nrm[0:1, col * 8:col * 8 + 1],
                    in1=nrm[0:1, 32 + col * 8:33 + col * 8], op=OP.max),
                    reads=("nrm",), writes=("nrm",))
        sch.op("dve", lambda v: v.tensor_tensor(
            out=nrm[0:1, 16:17], in0=nrm[0:1, 0:1], in1=nrm[0:1, 8:9], op=OP.mult),
            reads=("nrm",), writes=("nrm",))
        sch.op("act", lambda s_: s_.activation(
            out=nrm[0:1, 16:17], in_=nrm[0:1, 16:17], func=AF.Ln),
            reads=("nrm",), writes=("nrm",))
        sch.op("act", lambda s_: s_.activation(
            out=nrm[0:1, 24:25], in_=nrm[0:1, 16:17], func=AF.Exp, scale=0.5),
            reads=("nrm",), writes=("nrm",))
        sch.op("dve", lambda v: v.tensor_scalar(
            out=nrm[0:1, 24:25], in0=nrm[0:1, 24:25], scalar1=-SCALE_ATT,
            scalar2=None, op0=OP.mult), reads=("nrm",), writes=("nrm",))
        sch.dma(scr_mh, nrm[0:1, 24:32], "sc", reads=("nrm",), writes=("scr_mh",))
        sch.dma(mhat, scr_mh[0:1, 0:1].partition_broadcast(128).squeeze(1),
                "sc", reads=("scr_mh",), writes=("mhat",))

    def dump(ap, keys, nrows=128):
        N = ap.shape[1]
        for c0 in range(0, min(N, 32768), 1024):
            cw = min(1024, N - c0)
            hh = (c0 // 1024) % 2
            sl = xstage[:, hh * 1024:hh * 1024 + cw]
            sch.op("dve", lambda v, sl=sl, ap=ap, c0=c0, cw=cw: v.tensor_copy(
                out=sl, in_=ap[:, c0:c0 + cw]), reads=keys, writes=(f"os{hh}",))
            r0 = (c0 // 1024) * 128
            cw2 = max(cw, 8)
            sch.dma(dout[r0:r0 + 128, 0:cw2], xstage[:, hh * 1024:hh * 1024 + cw2],
                    "out", reads=(f"os{hh}",))

    def attention(s):
        so = 1 - s
        sch.alias("wst0", "xs0", "xs1", "fo0", "fo1")
        sch.alias("wst1", "xs0", "xs1", "fo0", "fo1")
        spill_keys = tuple(f"res{t}" for t in range(so * NTCH, (so + 1) * NTCH))
        for t in range(so * NTCH, (so + 1) * NTCH):
            tt = t - so * NTCH
            sch.dma(scr_res[tt * 128:(tt + 1) * 128, :], res[:, t * D:(t + 1) * D],
                    "sc", reads=(f"res{t}",), writes=(f"scrres{tt}",))
        rh = AB + so * 32768
        a_wqr = nc.alloc_sbuf_tensor_at(f"a_wqr{s}", [128, 4096], BF16, offset=rh).ap()
        a_ctk = nc.alloc_sbuf_tensor_at(f"a_ctk{s}", [128, 8192], BF16, offset=rh).ap()
        a_wuv = nc.alloc_sbuf_tensor_at(f"a_wuv{s}", [128, 4096], BF16,
                                        offset=rh + 16384).ap()
        a_wo = nc.alloc_sbuf_tensor_at(f"a_wo{s}", [128, 8192], BF16, offset=rh).ap()
        a_ao = nc.alloc_sbuf_tensor_at(f"a_ao{s}", [128, 4096], BF16,
                                       offset=rh + 16384).ap()
        a_pt = nc.alloc_sbuf_tensor_at(f"a_pt{s}", [128, 2048], BF16,
                                       offset=rh + 24576).ap()
        KLO, KHI, KPT = f"rhlo{s}", f"rhhi{s}", f"rhpt{s}"
        sch.alias(KLO, *spill_keys)
        sch.alias(KHI, *spill_keys)
        sch.alias(KPT, *spill_keys)

        sch.alias("xnT", "V65", "rows_f")
        sch.alias("lntile", "rsbs")
        sch.alias("lntile2", "ropet")
        for c in range(NTCH):
            lt = lntile if c % 2 == 0 else lntile2
            lk = "lntile" if c % 2 == 0 else "lntile2"
            ln_chunk(s * NTCH + c, c, lt, lk)
            transp8(lt, lk,
                    lambda j, c=c: xnT[:, j * S + c * 128:j * S + c * 128 + 128],
                    "xnT")
        if TAP[0] and TAP[0][0] == f"a0_{s}":
            dump(xnT, ("xnT",))
            return
        sch.alias("ropet", "lntile2")
        load_w(dw["Wd_q_w"], w_a, "w_a")
        load_w(dw["Wu_q_w"], w_b, "w_b")
        load_w(dw["Wq_r_w"], a_wqr, KLO)
        for c in range(NTCH):
            sch.dma(csbuf[:, 0:512], dw["cos_t"][c * 128:(c + 1) * 128, :], "cs0",
                    writes=("csb",))
            sch.dma(csbuf[:, 512:1024], dw["sin_t"][c * 128:(c + 1) * 128, :], "cs1",
                    writes=("csb",))
            for m in range(4):
                bk = 4 + m % 2
                po = pb[bk][:, 0:128]
                for k in range(8):
                    sch.op("pe", lambda T, m=m, k=k, c=c, po=po: T.matmul(
                        out=po, lhsT=w_a[:, k * DQ + m * 128:k * DQ + m * 128 + 128],
                        rhs=xnT[:, k * S + c * 128:k * S + c * 128 + 128],
                        start=(k == 0), stop=(k == 7)),
                        reads=("w_a", "xnT"), writes=(f"ps{bk}",))
                sch.op("dve", lambda v, m=m, po=po: v.tensor_scalar(
                    out=ctq_c[:, m * 128:(m + 1) * 128], in0=po,
                    scalar1=cbq[:, m:m + 1], scalar2=None, op0=OP.add),
                    reads=(f"ps{bk}", "consts"), writes=("ctq_c",))
            for half in range(2):
                for wsb, wk, rname, bank in ((w_b, "w_b", "buq", half),
                                             (a_wqr, KLO, "bqr", 2 + half)):
                    for k in range(4):
                        sch.op("pe", lambda T, k=k, half=half, wsb=wsb, bank=bank:
                               T.matmul(out=pb[bank],
                                        lhsT=ctq_c[:, k * 128:(k + 1) * 128],
                                        rhs=wsb[:, k * D + half * 512:k * D + half * 512 + 512],
                                        start=(k == 0), stop=False),
                               reads=("ctq_c", wk), writes=(f"ps{bank}",))
                    sch.op("pe", lambda T, half=half, rname=rname, bank=bank:
                           T.matmul(out=pb[bank], lhsT=ones_bf[0:1, :],
                                    rhs=row(rname, half * 512, half * 512 + 512),
                                    start=False, stop=True),
                           reads=("consts",), writes=(f"ps{bank}",))
            rope_psum(2)
            for half in range(2):
                sch.op("dve", lambda v, half=half: v.tensor_tensor(
                    out=lntile[:, half * 512:half * 512 + 512], in0=pb[half],
                    in1=ropet[:, half * 512:half * 512 + 512], op=OP.add),
                    reads=(f"ps{half}", "ropet"), writes=("lntile",))
            transp8(lntile, "lntile",
                    lambda j, c=c: QT[:, j * S + c * 128:j * S + c * 128 + 128],
                    "QT", bank=6, bank2=7)
        if TAP[0] and TAP[0][0] == f"qt_{s}":
            dump(QT, ("QT",))
            return
        sch.alias("w_ab", "w_a", "w_b")
        load_w(dw["Wk_r_w"], w_ab, "w_ab")
        for c in range(NTCH):
            sch.dma(csbuf[:, 0:512], dw["cos_t"][c * 128:(c + 1) * 128, :], "cs0",
                    writes=("csb",))
            sch.dma(csbuf[:, 512:1024], dw["sin_t"][c * 128:(c + 1) * 128, :], "cs1",
                    writes=("csb",))
            for half in range(2):
                for k in range(8):
                    sch.op("pe", lambda T, k=k, half=half, c=c: T.matmul(
                        out=pb[2 + half],
                        lhsT=xnT[:, k * S + c * 128:k * S + c * 128 + 128],
                        rhs=w_ab[:, k * D + half * 512:k * D + half * 512 + 512],
                        start=(k == 0), stop=False),
                        reads=("xnT", "w_ab"), writes=(f"ps{2 + half}",))
                sch.op("pe", lambda T, half=half: T.matmul(
                    out=pb[2 + half], lhsT=ones_bf[0:1, :],
                    rhs=row("bkr", half * 512, half * 512 + 512),
                    start=False, stop=True),
                    reads=("consts",), writes=(f"ps{2 + half}",))
            rope_psum(2)
            transp8(ropet, "ropet",
                    lambda j, c=c: KT[:, j * S + c * 128:j * S + c * 128 + 128],
                    "KT", bank=5, bank2=6)
        sch.alias(KLO, "w_ab")
        load_w(dw["Wd_kv_w"], w_a, "w_a")
        for c in range(NTCH):
            for m in range(4):
                bk = 4 + m % 2
                po = pb[bk][:, 0:128]
                for k in range(8):
                    sch.op("pe", lambda T, m=m, k=k, c=c, po=po: T.matmul(
                        out=po, lhsT=w_a[:, k * DKV + m * 128:k * DKV + m * 128 + 128],
                        rhs=xnT[:, k * S + c * 128:k * S + c * 128 + 128],
                        start=(k == 0), stop=(k == 7)),
                        reads=("w_a", "xnT"), writes=(f"ps{bk}",))
                sch.op("dve", lambda v, m=m, c=c, po=po: v.tensor_scalar(
                    out=a_ctk[:, m * S + c * 128:m * S + c * 128 + 128], in0=po,
                    scalar1=cbkv[:, m:m + 1], scalar2=None, op0=OP.add),
                    reads=(f"ps{bk}", "consts"), writes=(KLO,))
        load_w(dw["Wu_k_w"], w_b, "w_b")
        for m in range(8):
            for tb in range(4):
                bank = tb % 2
                for k in range(4):
                    sch.op("pe", lambda T, m=m, tb=tb, k=k, bank=bank: T.matmul(
                        out=pb[bank],
                        lhsT=w_b[:, k * D + m * 128:k * D + m * 128 + 128],
                        rhs=a_ctk[:, k * S + tb * 512:k * S + tb * 512 + 512],
                        start=(k == 0), stop=(k == 3)),
                        reads=("w_b", KLO), writes=(f"ps{bank}",))
                sch.op("dve", lambda v, m=m, tb=tb, bank=bank: v.scalar_tensor_tensor(
                    out=KT[:, m * S + tb * 512:m * S + tb * 512 + 512],
                    in0=pb[bank], scalar=cbuk[:, m:m + 1],
                    in1=KT[:, m * S + tb * 512:m * S + tb * 512 + 512],
                    op0=OP.add, op1=OP.add),
                    reads=(f"ps{bank}", "KT", "consts"), writes=("KT",))
        if TAP[0] and TAP[0][0] == f"kt_{s}":
            dump(KT, ("KT",))
            return
        load_w(dw["Wu_v_w"], a_wuv, KHI)
        sch.alias("V65", "xnT")
        for c in range(NTCH):
            for half in range(2):
                for k in range(4):
                    sch.op("pe", lambda T, k=k, half=half, c=c: T.matmul(
                        out=pb[2 + half],
                        lhsT=a_ctk[:, k * S + c * 128:k * S + c * 128 + 128],
                        rhs=a_wuv[:, k * D + half * 512:k * D + half * 512 + 512],
                        start=(k == 0), stop=False),
                        reads=(KLO, KHI), writes=(f"ps{2 + half}",))
                sch.op("pe", lambda T, half=half: T.matmul(
                    out=pb[2 + half], lhsT=ones_bf[0:1, :],
                    rhs=row("buv", half * 512, half * 512 + 512),
                    start=False, stop=True),
                    reads=("consts",), writes=(f"ps{2 + half}",))
            base = c * H * 65
            for h in range(H):
                bank = 2 + h // 8
                sch.op("dve", lambda v, h=h, bank=bank, base=base: v.tensor_copy(
                    out=V65[:, base + h * 65:base + h * 65 + 64],
                    in_=pb[bank][:, (h % 8) * 64:(h % 8) * 64 + 64]),
                    reads=(f"ps{bank}",), writes=("V65",))
            sch.op("dve", lambda v, base=base: v.memset(
                V65[:, base + 64:base + H * 65:65], 1.0), writes=("V65",))
        for kk in ("sqb", "rs_bf", "rbb"):
            sch.alias(kk, "w_a", "w_b", "w_ab", "wst0", "wst1")
        if TAP[0] and TAP[0][0] == f"v_{s}":
            dump(V65, ("V65",))
            return
        norm_bound()
        if TAP[0] and TAP[0][0] == f"nb_{s}":
            dump(nrm, ("nrm", "mhat"))
            return
        if TAP[0] and TAP[0][0] == f"mh_{s}":
            dump(mhat, ("mhat",))
            return
        sch.alias("rsbs", "lntile")
        sch.alias(KLO, KLO)
        load_w(dw["Wo_w"], a_wo, KLO)
        sch.alias(KHI, KHI)
        for qb in range(4):
            core_qblock(s, qb, a_wo, a_ao, a_pt, KLO, KHI, KPT)
            if TAP[0] and TAP[0][0] == f"core{qb}_{s}":
                dump(a_ao, (KHI,))
                return
            if TAP[0] and TAP[0][0] == f"pt{qb}_{s}":
                dump(a_pt, ("PT0", "PT1"))
                return
        for t in range(so * NTCH, (so + 1) * NTCH):
            tt = t - so * NTCH
            sch.dma(res[:, t * D:(t + 1) * D], scr_res[tt * 128:(tt + 1) * 128, :],
                    "sc2", reads=(f"scrres{tt}",), writes=(f"res{t}", KLO, KHI, KPT))

    def core_qblock(s, qb, a_wo, a_ao, a_pt, KLO, KHI, KPT):
        def emit_S(hp, kc):
            bsel = kc % 2
            for sub, hh in enumerate((2 * hp, 2 * hp + 1)):
                stile = pb[2 * sub + bsel]
                skey = f"ps{2 * sub + bsel}"
                base = (hh % 2) * 64
                fc = hp
                sch.op("pe", lambda T, kc=kc, qb=qb, stile=stile, fc=fc,
                       base=base: T.matmul(
                    out=stile,
                    lhsT=KT[base:base + 64, fc * S + kc * 128:fc * S + kc * 128 + 128],
                    rhs=QT[base:base + 64, fc * S + qb * 512:fc * S + qb * 512 + 512],
                    start=True, stop=True, tile_position=(base, 0)),
                    reads=("KT", "QT"), writes=(skey,))

        def emit_exp(hp, kc):
            bsel = kc % 2
            for sub in range(2):
                stile = pb[2 * sub + bsel]
                skey = f"ps{2 * sub + bsel}"
                pt = a_pt[:, (2 * sub + bsel) * 512:(2 * sub + bsel) * 512 + 512]
                ptk = f"PT{2 * sub + bsel}"
                sch.op("act", lambda s_, stile=stile, pt=pt: s_.activation(
                    out=pt, in_=stile, func=AF.Exp, bias=mhat, scale=SCALE_ATT),
                    reads=(skey, "mhat"), writes=(ptk,))

        def emit_AV(hp, kc):
            bsel = kc % 2
            for sub, hh in enumerate((2 * hp, 2 * hp + 1)):
                pt = a_pt[:, (2 * sub + bsel) * 512:(2 * sub + bsel) * 512 + 512]
                ptk = f"PT{2 * sub + bsel}"
                ob = pb[4 + sub]
                sch.op("pe", lambda T, hh=hh, kc=kc, ob=ob, pt=pt: T.matmul(
                    out=ob[0:65, :],
                    lhsT=V65[:, kc * H * 65 + hh * 65:kc * H * 65 + hh * 65 + 65],
                    rhs=pt, start=(kc == 0), stop=(kc == NTCH - 1)),
                    reads=("V65", ptk), writes=(f"ps{4 + sub}",))

        for hp in range(8):
            emit_S(hp, 0)
            for kc in range(NTCH):
                emit_exp(hp, kc)
                if kc + 1 < NTCH:
                    emit_S(hp, kc + 1)
                emit_AV(hp, kc)
            for sub, hh in enumerate((2 * hp, 2 * hp + 1)):
                ob = pb[4 + sub]
                okey = f"ps{4 + sub}"
                lnf = lntile.bitcast(F32)
                sch.op("act", lambda s_, ob=ob, lnf=lnf: s_.activation(
                    out=lnf[64:65, 0:512], in_=ob[64:65, :], func=AF.Ln),
                    reads=(okey,), writes=("rsbs",))
                sch.op("act", lambda s_, lnf=lnf: s_.activation(
                    out=rs_bf[64:65, :], in_=lnf[64:65, 0:512], func=AF.Exp, scale=-1.0),
                    reads=("rsbs",), writes=("rs_bf",))
                sch.op("pe", lambda T: T.matmul(
                    out=pb[6][0:64, :], lhsT=ones_bf[64:65, 0:64],
                    rhs=rs_bf[64:65, :], start=True, stop=True,
                    tile_position=(64, 0)),
                    reads=("rs_bf", "consts"), writes=("ps6",))
                sch.op("dve", lambda v: v.tensor_copy(out=rbb[0:64, :],
                                                      in_=pb[6][0:64, :]),
                       reads=("ps6",), writes=("rbb",))
                base = (hh % 2) * 64
                fc = hp
                sch.op("dve", lambda v, ob=ob, base=base, fc=fc: v.tensor_tensor(
                    out=a_ao[base:base + 64, fc * 512:fc * 512 + 512],
                    in0=ob[0:64, :], in1=rbb[0:64, :], op=OP.mult),
                    reads=(okey, "rbb"), writes=(KHI,))
        for tc in range(4):
            t = s * NTCH + qb * 4 + tc
            for half in range(2):
                bank = 6 + half
                for k in range(8):
                    sch.op("pe", lambda T, k=k, tc=tc, half=half, bank=bank: T.matmul(
                        out=pb[bank],
                        lhsT=a_ao[:, k * 512 + tc * 128:k * 512 + tc * 128 + 128],
                        rhs=a_wo[:, k * D + half * 512:k * D + half * 512 + 512],
                        start=(k == 0), stop=False),
                        reads=(KHI, KLO), writes=(f"ps{bank}",))
                sch.op("pe", lambda T, half=half, bank=bank: T.matmul(
                    out=pb[bank], lhsT=ones_bf[0:1, :],
                    rhs=row("bo", half * 512, half * 512 + 512),
                    start=False, stop=True), reads=("consts",), writes=(f"ps{bank}",))
                sch.op("dve", lambda v, t=t, half=half, bank=bank: v.tensor_tensor(
                    out=res[:, t * D + half * 512:t * D + half * 512 + 512],
                    in0=pb[bank],
                    in1=res[:, t * D + half * 512:t * D + half * 512 + 512],
                    op=OP.add), reads=(f"ps{bank}", f"res{t}"), writes=(f"res{t}",))

    def gate_ops(t):
        gsl = f_g[:, t * 8:(t + 1) * 8]
        lg = glg
        m1 = gm1
        m1n = gm1n
        m2 = gm2
        zz = gzz
        zr = gzr
        tA = gtA
        tB = gtB
        sch.op("dve", lambda v: v.tensor_copy(out=lg, in_=pb[6][:, 0:8]),
               reads=("ps6",), writes=("glg",))
        sch.op("dve", lambda v: v.tensor_reduce(
            out=m1, in_=lg, axis=AX.X, op=OP.max), reads=("glg",), writes=("gm1",))
        sch.op("dve", lambda v: v.tensor_scalar(
            out=tA, in0=lg, scalar1=m1, scalar2=-1e30, op0=OP.is_equal,
            op1=OP.mult), reads=("glg", "gm1"), writes=("gtA",))
        sch.op("dve", lambda v: v.tensor_tensor(
            out=tB, in0=tA, in1=lg, op=OP.add), reads=("gtA", "glg"),
            writes=("gtB",))
        sch.op("dve", lambda v: v.tensor_reduce(
            out=m2, in_=tB, axis=AX.X, op=OP.max), reads=("gtB",), writes=("gm2",))
        sch.op("dve", lambda v: v.tensor_scalar(
            out=m1n, in0=m1, scalar1=-1.0, scalar2=None, op0=OP.mult),
            reads=("gm1",), writes=("gm1",))
        sch.op("act", lambda s_: s_.activation(
            out=tA, in_=lg, func=AF.Exp, bias=m1n),
            reads=("glg", "gm1", "gtA"), writes=("gtA",))
        sch.op("dve", lambda v: v.scalar_tensor_tensor(
            out=tB, in0=lg, scalar=m2, in1=tA, op0=OP.is_ge, op1=OP.mult),
            reads=("glg", "gm2", "gtA"), writes=("gtB",))
        sch.op("dve", lambda v: v.tensor_reduce(
            out=zz, in_=tB, axis=AX.X, op=OP.add), reads=("gtB",), writes=("gz",))
        sch.op("act", lambda s_: s_.activation(out=zr, in_=zz, func=AF.Ln),
               reads=("gz",), writes=("gz",))
        sch.op("act", lambda s_: s_.activation(out=zr, in_=zr, func=AF.Exp,
                                               scale=-1.0),
               reads=("gz",), writes=("gz",))
        dbg = TAP[0][1] if TAP[0] else None
        if dbg == "lg":
            sch.op("dve", lambda v, gsl=gsl: v.tensor_copy(out=gsl, in_=lg),
                   reads=("glg",), writes=("f_gk",))
        elif dbg == "stats":
            sch.op("dve", lambda v, gsl=gsl: v.tensor_copy(out=gsl[:, 0:1], in_=m1),
                   reads=("gm1",), writes=("f_gk",))
            sch.op("dve", lambda v, gsl=gsl: v.tensor_copy(out=gsl[:, 1:2], in_=m2),
                   reads=("gm2",), writes=("f_gk",))
            sch.op("dve", lambda v, gsl=gsl: v.tensor_copy(out=gsl[:, 2:3], in_=zz),
                   reads=("gz",), writes=("f_gk",))
            sch.op("dve", lambda v, gsl=gsl: v.tensor_copy(out=gsl[:, 3:4], in_=zr),
                   reads=("gz",), writes=("f_gk",))
            sch.op("dve", lambda v, gsl=gsl: v.tensor_copy(out=gsl[:, 4:8], in_=tB[:, 0:4]),
                   reads=("gtB",), writes=("f_gk",))
        else:
            sch.op("dve", lambda v, gsl=gsl: v.tensor_scalar(
                out=gsl, in0=tB, scalar1=zr, scalar2=None, op0=OP.mult),
                reads=("gtB", "gz"), writes=("f_gk",))

    def ffn(final):
        NT = SEQ * NTCH
        mkeys = ("m_xnT", "m_xz", "m_dt", "m_xc", "m_sg", "m_v", "m_ye", "m_yo",
                 "m_bcB0", "m_bcB1", "m_bcC0", "m_bcC1", "m_dA0", "m_dA1",
                 "m_dA2", "m_dA3", "m_dAi0", "m_dAi1", "m_dAi2", "m_dAi3",
                 "m_sp0", "m_sp1", "m_sp2", "m_cacc", "m_h0", "m_h1", "m_h2",
                 "m_h3", "m_dbx0", "m_dbx1", "m_dbx2", "m_dbx3",
                 "m_carry0", "m_carry1", "m_carry2", "m_carry3", "m_bmcm", "m_bmT",
                 "m_halo", "m_mini", "m_wk0", "m_wk1", "m_wk2", "m_wk3", "m_bcw")
        sch.alias("f_xnT", "QT", "KT", "xnT", "V65", *mkeys)
        for fk in ("f_w1", "f_w2", "f_wg", "f_g", "f_gh", "f_ghT", "f_gk"):
            sch.alias(fk, "xnT", "V65", "rs_bf", "rbb", "sqb", *mkeys)
        sch.alias("geh", "lntile")
        sch.alias("lntile", "rsbs")
        sch.alias("wst0", "fo0_0", "fo0_1", "fo1_0", "fo1_1")
        sch.alias("wst1", "fo0_0", "fo0_1", "fo1_0", "fo1_1")
        load_w(dw["W1r"], f_w1, "f_w1", role="wf")
        load_w(dw["W2s"], f_w2, "f_w2", role="wf")
        load_w(dw["WgT"], f_wg, "f_wg", role="wf")
        for t in range(NT):
            ln_chunk(t, t % NTCH, lntile, "lntile")
            transp8(lntile, "lntile",
                    lambda j, t=t: f_xnT[:, j * NT * 128 + t * 128:
                                         j * NT * 128 + t * 128 + 128], "f_xnT",
                    bank=4, bank2=5)
            for k in range(8):
                sch.op("pe", lambda T, k=k, t=t: T.matmul(
                    out=pb[6][:, 0:8],
                    lhsT=f_xnT[:, k * NT * 128 + t * 128:k * NT * 128 + t * 128 + 128],
                    rhs=f_wg[:, k * 8:k * 8 + 8], start=(k == 0), stop=(k == 7)),
                    reads=("f_xnT", "f_wg"), writes=("ps6",))
            sch.op("pe", lambda T: T.matmul(
                out=pb[6][:, 0:8], lhsT=ones_bf[0:1, :], rhs=row("bg"),
                start=False, stop=True), reads=("consts",), writes=("ps6",))
            gate_ops(t)
        if TAP[0] and TAP[0][0] == "gates":
            dump(f_g, ("f_gk",))
            return
        for t in range(NT):
            gsl = f_g[:, t * 8:(t + 1) * 8]
            ph = t % 2
            hb = 0 if ph == 0 else 4
            tb_ = 1 if ph == 0 else 5
            ge = lntile if ph == 0 else lntile2
            gek = "geh" if ph == 0 else "geh2"
            ghk = "f_gh" if ph == 0 else "f_gh2"
            ghTk = "f_ghT" if ph == 0 else "f_ghT2"
            gh = f_gh[:, 0:256] if False else (f_gh if ph == 0 else f_ghT)
            for k in range(8):
                sch.op("pe", lambda T, k=k, t=t, hb=hb: T.matmul(
                    out=pb[hb],
                    lhsT=f_xnT[:, k * NT * 128 + t * 128:k * NT * 128 + t * 128 + 128],
                    rhs=f_w1[:, k * E * R:(k + 1) * E * R],
                    start=(k == 0), stop=(k == 7)),
                    reads=("f_xnT", "f_w1"), writes=(f"ps{hb}",))
            sch.op("act", lambda s_, hb=hb, ge=ge: s_.activation(
                out=ge[:, 0:512], in_=pb[hb], func=AF.Gelu),
                reads=(f"ps{hb}",), writes=(gek,))
            for e in range(E):
                sch.op("dve", lambda v, e=e, gsl=gsl, ge=ge, gh=gh: v.tensor_scalar(
                    out=gh[:, e * R:(e + 1) * R], in0=ge[:, e * R:(e + 1) * R],
                    scalar1=gsl[:, e:e + 1], scalar2=None, op0=OP.mult),
                    reads=(gek, "f_gk"), writes=(ghk,))
            for j in range(4):
                sl = pbh[tb_][:, j * 128:j * 128 + 128]
                sch.op("pe", lambda T, j=j, sl=sl, gh=gh: T.transpose(
                    out=sl, in_=gh[:, j * 128:(j + 1) * 128], identity=ident),
                    reads=(ghk, "consts"), writes=(f"ps{tb_}",))
                sch.op("dve", lambda v, j=j, sl=sl, gh=gh, ph=ph: v.tensor_copy(
                    out=(f_gh if ph == 1 else f_ghT)[:, j * 128:(j + 1) * 128], in_=sl),
                    reads=(f"ps{tb_}",), writes=(ghTk,))
            for half in range(2):
                bank = (2 + half) if ph == 0 else (6 + half)
                ght = f_ghT if ph == 0 else f_gh
                for k in range(4):
                    sch.op("pe", lambda T, k=k, half=half, bank=bank, ght=ght: T.matmul(
                        out=pb[bank], lhsT=ght[:, k * 128:(k + 1) * 128],
                        rhs=f_w2[:, k * D + half * 512:k * D + half * 512 + 512],
                        start=(k == 0), stop=(k == 3)),
                        reads=(ghTk, "f_w2"), writes=(f"ps{bank}",))
                if not final:
                    sch.op("dve", lambda v, t=t, half=half, bank=bank:
                           v.scalar_tensor_tensor(
                               out=res[:, t * D + half * 512:t * D + half * 512 + 512],
                               in0=pb[bank], scalar=2.0,
                               in1=res[:, t * D + half * 512:t * D + half * 512 + 512],
                               op0=OP.mult, op1=OP.add),
                           reads=(f"ps{bank}", f"res{t}"), writes=(f"res{t}",))
                else:
                    sch.op("dve", lambda v, t=t, half=half, bank=bank, ph=ph:
                           v.scalar_tensor_tensor(
                               out=xstage[:, ph * 1024 + half * 512:
                                          ph * 1024 + (half + 1) * 512],
                               in0=pb[bank], scalar=2.0,
                               in1=res[:, t * D + half * 512:t * D + half * 512 + 512],
                               op0=OP.mult, op1=OP.add),
                           reads=(f"ps{bank}", f"res{t}"), writes=(f"fo{t % 2}_{half}",))
            if final:
                ph = t % 2
                sch.dma(dout[t * 128:(t + 1) * 128, :],
                        xstage[:, ph * 1024:(ph + 1) * 1024], "out",
                        reads=(f"fo{ph}_0", f"fo{ph}_1"))

    def softplus_evac(m, bank):
        x = pb[bank]
        t0 = m_sp[:, 0:TCH]
        t1 = m_sp[:, TCH:2 * TCH]
        dst = m_dt[:, m * TCH:(m + 1) * TCH]
        sch.op("dve", lambda v, x=x, m=m, t0=t0: v.tensor_scalar(
            out=t0, in0=x, scalar1=cdtb[:, m:m + 1], scalar2=None, op0=OP.add),
            reads=(f"ps{bank}", "consts"), writes=("m_sp0",))
        sch.op("act", lambda s_, t0=t0, t1=t1: s_.activation(
            out=t1, in_=t0, func=AF.Abs),
            reads=("m_sp0",), writes=("m_sp1",))
        sch.op("act", lambda s_, t1=t1: s_.activation(
            out=t1, in_=t1, func=AF.Exp, scale=-1.0),
            reads=("m_sp1",), writes=("m_sp1",))
        sch.op("act", lambda s_, t1=t1: s_.activation(
            out=t1, in_=t1, func=AF.Ln, bias=cone, scale=1.0),
            reads=("m_sp1", "consts"), writes=("m_sp1",))
        sch.op("dve", lambda v, t0=t0: v.tensor_scalar(
            out=t0, in0=t0, scalar1=0.0, scalar2=None, op0=OP.max),
            reads=("m_sp0",), writes=("m_sp0",))
        sch.op("dve", lambda v, t0=t0, t1=t1, dst=dst: v.tensor_tensor(
            out=dst, in0=t0, in1=t1, op=OP.add),
            reads=("m_sp0", "m_sp1"), writes=("m_dt",))

    def mamba_chunk(s, ci):
        last = (ci == NMCH - 1)
        c0 = ci * 4
        m_halo = m_haloA[:, s * NDC * 3:(s + 1) * NDC * 3]
        sch.alias("m_xz", "m_dt")
        sch.alias("m_cacc", "m_sp1", "m_sp2", "m_dAi0", "m_dAi1", "m_dAi2", "m_dAi3")
        for cc in range(4):
            t = s * NTCH + c0 + cc
            ln_chunk(t, cc, lntile, "lntile")
            transp8(lntile, "lntile",
                    lambda j, cc=cc: m_xnT[:, j * TCH + cc * 128:j * TCH + cc * 128 + 128],
                    "m_xnT")
        if not last:
            t = s * NTCH + c0 + 4
            ln_chunk(t, 4, lntile, "lntile")
            for j in range(8):
                sl = pbh[7][:, (j % 4) * 128:(j % 4) * 128 + 128]
                sch.op("pe", lambda T, j=j, sl=sl: T.transpose(
                    out=sl, in_=lntile[:, j * 128:j * 128 + 128], identity=ident),
                    reads=("lntile", "consts"), writes=("ps7",))
                sch.op("dve", lambda v, j=j, sl=sl: v.tensor_copy(
                    out=m_mini[:, j * 2:j * 2 + 2], in_=sl[:, 0:2]),
                    reads=("ps7",), writes=("m_mini",))
        for mg in range(4):
            for k in range(8):
                wb = m_wk[:, (k % 4) * 1024:(k % 4) * 1024 + 1024]
                wk = f"m_wk{k % 4}"
                sch.dma(wb, scr["in_w"][k * 128:(k + 1) * 128,
                                        mg * 1024:(mg + 1) * 1024], f"wm{k % 4}",
                        reads=("scr_in_w",), writes=(wk,))
                for m in range(8):
                    sch.op("pe", lambda T, k=k, m=m, wb=wb: T.matmul(
                        out=pb[m], lhsT=wb[:, m * 128:(m + 1) * 128],
                        rhs=m_xnT[:, k * TCH:(k + 1) * TCH],
                        start=(k == 0), stop=(k == 7)),
                        reads=(wk, "m_xnT"), writes=(f"ps{m}",))
            for m in range(8):
                gm = mg * 8 + m
                if gm < 16:
                    sch.op("dve", lambda v, gm=gm, m=m: v.tensor_scalar(
                        out=m_xz[:, gm * (TCH + 3) + 1:gm * (TCH + 3) + 1 + TCH],
                        in0=pb[m], scalar1=cinb[:, gm:gm + 1], scalar2=None,
                        op0=OP.add), reads=(f"ps{m}", "consts"), writes=("m_xz",))
                else:
                    tg_ = m_sp[:, 0:TCH]
                    sch.op("dve", lambda v, gm=gm, m=m, tg_=tg_: v.tensor_scalar(
                        out=tg_, in0=pb[m], scalar1=cinb[:, gm:gm + 1],
                        scalar2=None, op0=OP.add),
                        reads=(f"ps{m}", "consts"), writes=("m_sp0",))
                    sch.op("act", lambda s_, gm=gm, tg_=tg_: s_.activation(
                        out=m_sg[:, (gm - 16) * TCH:(gm - 15) * TCH], in_=tg_,
                        func=AF.Silu),
                        reads=("m_sp0",), writes=("m_sg",))
            if not last and mg < 2:
                for k in range(8):
                    wb = m_wk[:, (k % 2) * 1024:(k % 2) * 1024 + 1024]
                    wk = f"m_wk{k % 2}"
                    pass
        if not last:
            for k in range(8):
                wb = m_wk[:, (k % 4) * 1024:(k % 4) * 1024 + 1024]
                wk = f"m_wk{k % 4}"
                sch.dma(wb, scr["in_w"][k * 128:(k + 1) * 128, 0:1024],
                        f"wm{k % 4}", reads=("scr_in_w",), writes=(wk,))
                for m in range(8):
                    sch.op("pe", lambda T, k=k, m=m, wb=wb: T.matmul(
                        out=pb[m][:, 0:2],
                        lhsT=wb[:, m * 128:(m + 1) * 128],
                        rhs=m_mini[:, k * 2:k * 2 + 2],
                        start=(k == 0), stop=(k == 7)),
                        reads=(wk, "m_mini"), writes=(f"ps{m}",))
            for m in range(8):
                sch.op("dve", lambda v, m=m: v.tensor_scalar(
                    out=m_halo[:, m * 3 + 1:m * 3 + 3], in0=pb[m][:, 0:2],
                    scalar1=cinb[:, m:m + 1], scalar2=None, op0=OP.add),
                    reads=(f"ps{m}", "consts"), writes=("m_halo",))
            for k in range(8):
                wb = m_wk[:, (k % 4) * 1024:(k % 4) * 1024 + 1024]
                wk = f"m_wk{k % 4}"
                sch.dma(wb, scr["in_w"][k * 128:(k + 1) * 128, 1024:2048],
                        f"wm{k % 4}", reads=("scr_in_w",), writes=(wk,))
                for m in range(8):
                    sch.op("pe", lambda T, k=k, m=m, wb=wb: T.matmul(
                        out=pb[m][:, 0:2],
                        lhsT=wb[:, m * 128:(m + 1) * 128],
                        rhs=m_mini[:, k * 2:k * 2 + 2],
                        start=(k == 0), stop=(k == 7)),
                        reads=(wk, "m_mini"), writes=(f"ps{m}",))
            for m in range(8):
                mm = m + 8
                sch.op("dve", lambda v, m=m, mm=mm: v.tensor_scalar(
                    out=m_halo[:, mm * 3 + 1:mm * 3 + 3], in0=pb[m][:, 0:2],
                    scalar1=cinb[:, mm:mm + 1], scalar2=None, op0=OP.add),
                    reads=(f"ps{m}", "consts"), writes=("m_halo",))
        else:
            sch.op("dve", lambda v: v.memset(m_halo[:, 1:NDC * 3:3], 0.0),
                   writes=("m_halo",))
            sch.op("dve", lambda v: v.memset(m_halo[:, 2:NDC * 3:3], 0.0),
                   writes=("m_halo",))
        if ci == 0:
            sch.op("dve", lambda v: v.memset(m_halo[:, 0:NDC * 3:3], 0.0),
                   writes=("m_halo",))
        for m in range(NDC):
            sch.op("dve", lambda v, m=m: v.tensor_copy(
                out=m_xz[:, m * (TCH + 3):m * (TCH + 3) + 1],
                in_=m_halo[:, m * 3:m * 3 + 1]),
                reads=("m_halo",), writes=("m_xz",))
        for m in range(NDC):
            sch.op("dve", lambda v, m=m: v.tensor_copy(
                out=m_xz[:, m * (TCH + 3) + TCH + 1:m * (TCH + 3) + TCH + 3],
                in_=m_halo[:, m * 3 + 1:m * 3 + 3]),
                reads=("m_halo",), writes=("m_xz",))
        for dc in range(NDC):
            base = dc * (TCH + 3)
            full = m_xz[:, base:base + TCH + 3]
            w = lambda i, dc=dc: cconv[:, dc * 4 + i:dc * 4 + i + 1]
            acc = m_sp[:, 0:TCH]
            sch.op("dve", lambda v, full=full, w=w, acc=acc: v.tensor_scalar(
                out=acc, in0=full[:, 0:TCH], scalar1=w(0), scalar2=None,
                op0=OP.mult), reads=("m_xz", "consts"), writes=("m_cacc",))
            for i in range(1, 4):
                sch.op("dve", lambda v, full=full, w=w, acc=acc, i=i:
                       v.scalar_tensor_tensor(out=acc, in0=full[:, i:i + TCH],
                                              scalar=w(i), in1=acc,
                                              op0=OP.mult, op1=OP.add),
                       reads=("m_xz", "consts", "m_cacc"), writes=("m_cacc",))
            sch.op("dve", lambda v, dc=dc, acc=acc: v.tensor_scalar(
                out=acc, in0=acc, scalar1=cconvb[:, dc:dc + 1], scalar2=None,
                op0=OP.add),
                reads=("m_cacc", "consts"), writes=("m_cacc",))
            sch.op("act", lambda s_, dc=dc, acc=acc: s_.activation(
                out=m_xc[:, dc * TCH:(dc + 1) * TCH], in_=acc, func=AF.Silu),
                reads=("m_cacc",), writes=("m_xc",))
        if not last:
            for m in range(NDC):
                sch.op("dve", lambda v, m=m: v.tensor_copy(
                    out=m_halo[:, m * 3:m * 3 + 1],
                    in_=m_xz[:, m * (TCH + 3) + TCH:m * (TCH + 3) + TCH + 1]),
                    reads=("m_xz",), writes=("m_halo",))
        sch.dma(m_bcw, scr["BC_w"], "wmb",
                reads=("scr_BC_w",), writes=("m_bcw",))
        for tsub in range(4):
            for k in range(NDC):
                sch.op("pe", lambda T, tsub=tsub, k=k: T.matmul(
                    out=pb[3][:, 0:32],
                    lhsT=m_xc[:, k * TCH + tsub * 128:k * TCH + tsub * 128 + 128],
                    rhs=m_bcw[:, k * 32:(k + 1) * 32],
                    start=(k == 0), stop=(k == NDC - 1)),
                    reads=("m_bcw", "m_xc"), writes=("ps3",))
            sch.op("pe", lambda T: T.matmul(
                out=pb[3][:, 0:32], lhsT=ones_bf[0:1, :], rhs=row("bc"),
                start=False, stop=True), reads=("consts",), writes=("ps3",))
            sch.op("dve", lambda v, tsub=tsub: v.tensor_copy(
                out=m_bmcm[:, tsub * 32:(tsub + 1) * 32], in_=pb[3][:, 0:32]),
                reads=("ps3",), writes=("m_bmcm",))
            sch.op("pe", lambda T, tsub=tsub: T.transpose(
                out=pbh[2][0:32, 0:128],
                in_=m_bmcm[:, tsub * 32:(tsub + 1) * 32], identity=ident),
                reads=("m_bmcm", "consts"), writes=("ps2",))
            sch.op("dve", lambda v, tsub=tsub: v.tensor_copy(
                out=m_bmT[:, tsub * 128:(tsub + 1) * 128], in_=pbh[2][0:32, 0:128]),
                reads=("ps2",), writes=("m_bmT",))
        sch.dma(scr_bc, m_bmT, "bcw", reads=("m_bmT",), writes=("scr_bck",))
        sch.alias("m_dt", "m_xz")
        sch.alias("m_sp1", "m_cacc", "m_dAi0", "m_dAi1", "m_dAi2", "m_dAi3")
        sch.alias("m_sp2", "m_cacc", "m_dAi0", "m_dAi1", "m_dAi2", "m_dAi3")
        sch.alias("m_sp0", "m_cacc", "m_dAi0", "m_dAi1", "m_dAi2", "m_dAi3")
        for mg in range(2):
            for k in range(NDC):
                wb = m_wk[:, (k % 4) * 1024:(k % 4) * 1024 + 1024]
                wk = f"m_wk{k % 4}"
                sch.dma(wb, scr["dt_w"][k * 128:(k + 1) * 128,
                                        mg * 1024:(mg + 1) * 1024], f"wm{k % 4}",
                        reads=("scr_dt_w",), writes=(wk,))
                for m in range(8):
                    sch.op("pe", lambda T, k=k, m=m, wb=wb: T.matmul(
                        out=pb[m], lhsT=wb[:, m * 128:(m + 1) * 128],
                        rhs=m_xc[:, k * TCH:(k + 1) * TCH],
                        start=(k == 0), stop=(k == NDC - 1)),
                        reads=(wk, "m_xc"), writes=(f"ps{m}",))
            for m in range(8):
                softplus_evac(mg * 8 + m, m)
        scan(s)
        out_proj(s, ci)

    def scan(s):
        dAi = m_sp.bitcast(BF16)

        def bc_fetch(si):
            sch.dma(m_bc[:, (si % 2) * 2 * TCH:(si % 2) * 2 * TCH + TCH],
                    scr_bc[si:si + 1, :].partition_broadcast(128).squeeze(1),
                    f"bc{si % 2}", reads=("scr_bck",), writes=(f"m_bcB{si % 2}",))
            sch.dma(m_bc[:, (si % 2) * 2 * TCH + TCH:(si % 2) * 2 * TCH + 2 * TCH],
                    scr_bc[16 + si:17 + si, :].partition_broadcast(128).squeeze(1),
                    f"bc{si % 2}", reads=("scr_bck",), writes=(f"m_bcC{si % 2}",))

        def emit_dA(dcb, si):
            for j in range(4):
                dc = dcb * 4 + j
                ai = dAi[:, j * TCH:(j + 1) * TCH]
                sch.op("dve", lambda v, dc=dc, si=si, ai=ai: v.tensor_scalar(
                    out=ai, in0=m_dt[:, dc * TCH:(dc + 1) * TCH],
                    scalar1=cA[:, dc * DS + si:dc * DS + si + 1],
                    scalar2=None, op0=OP.mult),
                    reads=("m_dt", "consts"), writes=(f"m_dAi{j}",))
            sch.op("act", lambda s_: s_.activation(
                out=m_dA[:, 0:4 * TCH], in_=dAi[:, 0:4 * TCH], func=AF.Exp),
                reads=("m_dAi0", "m_dAi1", "m_dAi2", "m_dAi3"),
                writes=("m_dA0", "m_dA1", "m_dA2", "m_dA3"))

        for dcb in range(4):
            sch.alias("m_dAi0", "m_sp0", "m_sp1", "m_cacc")
            for j in range(4):
                dc = dcb * 4 + j
                sch.op("dve", lambda v, dc=dc, j=j: v.tensor_tensor(
                    out=m_v[:, j * TCH:(j + 1) * TCH],
                    in0=m_dt[:, dc * TCH:(dc + 1) * TCH],
                    in1=m_xc[:, dc * TCH:(dc + 1) * TCH], op=OP.mult),
                    reads=("m_dt", "m_xc"), writes=("m_v",))
                sch.op("dve", lambda v, j=j: v.memset(
                    m_ye[:, j * TCH:(j + 1) * TCH], 0.0), writes=("m_ye",))
                sch.op("dve", lambda v, j=j: v.memset(
                    m_yo[:, j * TCH:(j + 1) * TCH], 0.0), writes=("m_yo",))
            bc_fetch(0)
            bc_fetch(1)
            emit_dA(dcb, 0)
            for si in range(DS):
                bs = si % 2
                bcB = m_bc[:, bs * 2 * TCH:bs * 2 * TCH + TCH]
                bcC = m_bc[:, bs * 2 * TCH + TCH:bs * 2 * TCH + 2 * TCH]
                for j in range(4):
                    dbx = m_dbx[:, j * TCH:(j + 1) * TCH]
                    sch.op("dve", lambda v, j=j, dbx=dbx, bcB=bcB: v.tensor_tensor(
                        out=dbx, in0=m_v[:, j * TCH:(j + 1) * TCH],
                        in1=bcB, op=OP.mult),
                        reads=("m_v", f"m_bcB{bs}"), writes=(f"m_dbx{j}",))
                for j in range(4):
                    dc = dcb * 4 + j
                    dA = m_dA[:, j * TCH:(j + 1) * TCH]
                    dbx = m_dbx[:, j * TCH:(j + 1) * TCH]
                    hb = m_h[:, j * TCH:(j + 1) * TCH]
                    cini = m_carry[:, s * NDC * DS + dc * DS + si:
                                   s * NDC * DS + dc * DS + si + 1]
                    sch.op("dve", lambda v, dA=dA, dbx=dbx, hb=hb, cini=cini:
                           v.tensor_tensor_scan(out=hb, data0=dA, data1=dbx,
                                                initial=cini, op0=OP.mult,
                                                op1=OP.add),
                           reads=(f"m_dA{j}", f"m_dbx{j}", f"m_carry{j}"),
                           writes=(f"m_h{j}",))
                warm = lambda rd: sch.op("pe", lambda T: T.matmul(
                    out=pb[7], lhsT=m_dA[:, 0:128], rhs=m_dA[:, 0:512],
                    start=True, stop=True), reads=rd, writes=("ps7",))
                if si + 1 < DS:
                    emit_dA(dcb, si + 1)
                    warm(("m_dA0",))
                else:
                    warm(())
                warm(())
                ypar = m_ye if si % 2 == 0 else m_yo
                ypk = "m_ye" if si % 2 == 0 else "m_yo"
                for j in range(4):
                    dc = dcb * 4 + j
                    dbx = m_dbx[:, j * TCH:(j + 1) * TCH]
                    hb = m_h[:, j * TCH:(j + 1) * TCH]
                    cini = m_carry[:, s * NDC * DS + dc * DS + si:
                                   s * NDC * DS + dc * DS + si + 1]
                    sch.op("dve", lambda v, hb=hb, cini=cini: v.tensor_copy(
                        out=cini, in_=hb[:, TCH - 1:TCH]),
                        reads=(f"m_h{j}",), writes=(f"m_carry{j}",))
                    sch.op("dve", lambda v, hb=hb, dbx=dbx, bcC=bcC: v.tensor_tensor(
                        out=dbx, in0=hb, in1=bcC, op=OP.mult),
                        reads=(f"m_h{j}", f"m_bcC{bs}"), writes=(f"m_dbx{j}",))
                for j in range(4):
                    dbx = m_dbx[:, j * TCH:(j + 1) * TCH]
                    sch.op("dve", lambda v, j=j, dbx=dbx, ypar=ypar: v.tensor_tensor(
                        out=ypar[:, j * TCH:(j + 1) * TCH],
                        in0=ypar[:, j * TCH:(j + 1) * TCH], in1=dbx, op=OP.add),
                        reads=(f"m_dbx{j}", ypk), writes=(ypk,))
                if si + 2 < DS:
                    bc_fetch(si + 2)
                    warm((f"m_bcB{si % 2}",))
                else:
                    warm(())
                warm(())
            for j in range(4):
                dc = dcb * 4 + j
                sg = m_sg[:, dc * TCH:(dc + 1) * TCH]
                sch.op("dve", lambda v, j=j: v.tensor_tensor(
                    out=m_ye[:, j * TCH:(j + 1) * TCH],
                    in0=m_ye[:, j * TCH:(j + 1) * TCH],
                    in1=m_yo[:, j * TCH:(j + 1) * TCH], op=OP.add),
                    reads=("m_ye", "m_yo"), writes=("m_ye",))
                sch.op("dve", lambda v, j=j, dc=dc: v.scalar_tensor_tensor(
                    out=m_ye[:, j * TCH:(j + 1) * TCH],
                    in0=m_xc[:, dc * TCH:(dc + 1) * TCH],
                    scalar=cDp[:, dc:dc + 1],
                    in1=m_ye[:, j * TCH:(j + 1) * TCH], op0=OP.mult, op1=OP.add),
                    reads=("m_xc", "consts", "m_ye"), writes=("m_ye",))
                sch.op("dve", lambda v, j=j, sg=sg: v.tensor_tensor(
                    out=sg, in0=m_ye[:, j * TCH:(j + 1) * TCH], in1=sg, op=OP.mult),
                    reads=("m_ye", "m_sg"), writes=("m_sg",))

    def out_proj(s, ci):
        for k in range(NDC):
            wb = m_wk[:, (k % 4) * 1024:(k % 4) * 1024 + 1024]
            wk = f"m_wk{k % 4}"
            sch.dma(wb, scr["out_w"][k * 128:(k + 1) * 128, :], f"wm{k % 4}",
                    reads=("scr_out_w",), writes=(wk,))
            for tsub in range(4):
                for half in range(2):
                    bank = 2 * tsub + half
                    sch.op("pe", lambda T, k=k, tsub=tsub, half=half, bank=bank,
                           wb=wb: T.matmul(
                        out=pb[bank],
                        lhsT=m_sg[:, k * TCH + tsub * 128:k * TCH + tsub * 128 + 128],
                        rhs=wb[:, half * 512:half * 512 + 512],
                        start=(k == 0), stop=False),
                        reads=(wk, "m_sg"), writes=(f"ps{bank}",))
        for tsub in range(4):
            t = s * NTCH + ci * 4 + tsub
            for half in range(2):
                bank = 2 * tsub + half
                sch.op("pe", lambda T, half=half, bank=bank: T.matmul(
                    out=pb[bank], lhsT=ones_bf[0:1, :],
                    rhs=row("outb", half * 512, half * 512 + 512),
                    start=False, stop=True), reads=("consts",), writes=(f"ps{bank}",))
                sch.op("dve", lambda v, t=t, half=half, bank=bank: v.tensor_tensor(
                    out=res[:, t * D + half * 512:t * D + half * 512 + 512],
                    in0=pb[bank],
                    in1=res[:, t * D + half * 512:t * D + half * 512 + 512],
                    op=OP.add), reads=(f"ps{bank}", f"res{t}"), writes=(f"res{t}",))

    def mamba():
        names = ("m_xnT m_xz m_dt m_xc m_sg m_v m_ye m_yo m_bcB0 m_bcB1 m_bcC0 "
                 "m_bcC1 m_dA0 m_dA1 m_dA2 m_dA3 m_dAi0 m_dAi1 m_dAi2 m_dAi3 "
                 "m_sp0 m_sp1 m_sp2 m_cacc m_h0 m_h1 m_h2 m_h3 "
                 "m_dbx0 m_dbx1 m_dbx2 m_dbx3 m_carry0 m_carry1 m_carry2 m_carry3 m_bmcm m_bmT m_halo "
                 "m_mini m_wk0 m_wk1 m_wk2 m_wk3 m_bcw").split()
        for nm in names:
            sch.alias(nm, "f_xnT", "f_w1", "f_w2", "f_wg", "f_gk", "f_gh", "f_ghT")
        sch.op("dve", lambda v: v.memset(m_carry, 0.0),
               writes=("m_carry0", "m_carry1", "m_carry2", "m_carry3"))
        for ci in range(NMCH):
            for s in range(SEQ):
                mamba_chunk(s, ci)

    def maybe_tap(phase):
        if TAP[0] and TAP[0][0] == phase:
            nm = TAP[0][1]
            obj = {"xnT": (xnT, ("xnT",)), "QT": (QT, ("QT",)),
                   "KT": (KT, ("KT",)), "V65": (V65, ("V65",)),
                   "res": (res, tuple(f"res{t}" for t in range(32))),
                   "mhat": (mhat, ("mhat",)), "nrm": (nrm, ("nrm",)),
                   "m_xz": (m_xz, ("m_xz",)), "m_xc": (m_xc, ("m_xc",)),
                   "m_dt": (m_dt, ("m_dt",)), "m_sg": (m_sg, ("m_sg",)),
                   "f_g": (f_g, ("f_gk",)),
                   }[nm]
            dump(obj[0], obj[1])
            return True
        return False

    if n_sub >= 3:
        cast_weights()
    tapped = False
    intap = lambda: TAP[0] and TAP[0][0].endswith(("_0", "_1"))
    if n_sub >= 1:
        attention(0)
        tapped = maybe_tap("attn0") or (TAP[0] and TAP[0][0].endswith("_0"))
        if not tapped:
            attention(1)
            tapped = maybe_tap("attn1") or (TAP[0] and TAP[0][0].endswith("_1"))
    if n_sub >= 2 and not tapped:
        ffn(final=False)
        tapped = tapped or maybe_tap("ffn1") or (TAP[0] and TAP[0][0] == "gates")
    if n_sub >= 3 and not tapped:
        mamba()
        tapped = maybe_tap("mamba")
    if tapped:
        pass
    elif n_sub >= 4:
        ffn(final=True)
    else:
        for t in range(SEQ * NTCH):
            hh = t % 2
            sl = xstage[:, hh * 1024:(hh + 1) * 1024]
            sch.op("dve", lambda v, sl=sl, t=t: v.tensor_copy(
                out=sl, in_=res[:, t * D:(t + 1) * D]),
                reads=(f"res{t}",), writes=(f"os{hh}",))
            sch.dma(dout[t * 128:(t + 1) * 128, :], sl, "out", reads=(f"os{hh}",))
    global LAST_SCH
    LAST_SCH = sch
    sch.finalize()
    return nc


def kernel(**inputs):
    g = host_prepare(inputs)
    x = np.ascontiguousarray(np.asarray(inputs["L0_attn_a_x"], np.float32))
    nc = build_program(4)
    in_maps = []
    for core in range(NCORE):
        m = dict(g)
        m["x"] = np.ascontiguousarray(x[2 * core:2 * core + 2].reshape(SEQ * S, D))
        in_maps.append(m)
    r = run_bass_kernel_spmd(nc, in_maps, core_ids=list(range(NCORE)))
    return np.concatenate([rr["out"].reshape(SEQ, S, D) for rr in r.results], axis=0)


# version 17 (final)
# speedup vs baseline: 1.0134x; 1.0134x over previous
import numpy as np
import ml_dtypes
import concourse.bass as bass
import concourse.mybir as mybir
from concourse.bass_utils import run_bass_kernel_spmd

F32 = mybir.dt.float32
BF16 = mybir.dt.bfloat16
OP = mybir.AluOpType
AF = mybir.ActivationFunctionType
AX = mybir.AxisListType

B, S, D = 16, 2048, 1024
NCORE = 8
SEQ = 2
NTCH = S // 128
H, HD = 16, 64
DQ = DKV = 512
E, R = 8, 64
DI, DS = 2048, 16
TCH = 512
NMCH = S // TCH
NDC = DI // 128
SCALE_ATT = float(1.0 / np.sqrt(16 + DQ + DKV))
EPS = 1e-5

ENGS = ("pe", "dve", "act", "pool", "sp")
DMA_ROLES = ("xin0", "xin1", "wa0", "wa1", "wf0", "wf1", "wm0", "wm1", "wm2", "wm3", "wmc", "wmb", "cst", "cs0", "cs1", "bc0", "bc1", "bcw", "sc", "sc2", "out")


class Sch:
    def __init__(self, nc):
        self.nc = nc
        self.sem = {e: nc.alloc_semaphore(f"sem_{e}") for e in ENGS}
        self.dsem = {r: nc.alloc_semaphore(f"dma_{r}") for r in DMA_ROLES}
        self.ops = {e: [] for e in ENGS}
        self.icnt = {e: 0 for e in ENGS}
        self.incs = {e: {} for e in ENGS}
        self.mlist = {e: [] for e in ENGS}
        self.mval = {e: 0 for e in ENGS}
        self.dissued = {r: 0 for r in DMA_ROLES}
        self.waited = {e: {} for e in ENGS}
        self.writer = {}
        self.readers = {}

    def _wait(self, eng, semobj, val, key):
        if self.waited[eng].get(key, 0) >= val:
            return
        self.ops[eng].append(("wait", semobj, val))
        self.waited[eng][key] = val

    def _milestone_for(self, e, idx):
        for cov, val in self.mlist[e]:
            if cov >= idx:
                return val
        last = self.icnt[e]
        assert last >= idx, f"milestone for {e}@{idx} but only {last} instrs"
        self.mval[e] += 1
        if e in ("dve", "pool"):
            for _ in range(3):
                self.ops[e].append(("call", lambda E: E.engine_nop()))
            self.icnt[e] = last + 3
            self.incs[e].setdefault(last + 2, []).append((self.sem[e], 1))
            self.mlist[e].append((last + 3, self.mval[e]))
        else:
            self.incs[e].setdefault(last - 1, []).append((self.sem[e], 1))
            self.mlist[e].append((last, self.mval[e]))
        return self.mval[e]

    def _need(self, eng, prod):
        kind, who, idx = prod
        if kind == "eng":
            if who == eng:
                return
            self._wait(eng, self.sem[who], self._milestone_for(who, idx), ("e", who))
        else:
            self._wait(eng, self.dsem[who], 16 * self.dissued[who], ("d", who))

    def _pre(self, eng, reads, writes):
        for b in reads:
            w = self.writer.get(b)
            if w is not None:
                self._need(eng, w)
        for b in writes:
            w = self.writer.get(b)
            if w is not None:
                self._need(eng, w)
            for r in self.readers.get(b, ()):
                self._need(eng, r)

    GAP_RAW = 4
    GAP_WAR = 2

    def _spacer(self, eng, n):
        for _ in range(n):
            self.ops[eng].append(("call", lambda E: E.engine_nop()))
            self.icnt[eng] += 1

    def op(self, eng, fn, reads=(), writes=()):
        self._pre(eng, reads, writes)
        if eng in ("dve", "pool"):
            need = 0
            for b in reads:
                w = self.writer.get(b)
                if w is not None and w[0] == "eng" and w[1] == eng:
                    need = max(need, self.GAP_RAW - (self.icnt[eng] - w[2]))
            for b in writes:
                w = self.writer.get(b)
                if w is not None and w[0] == "eng" and w[1] == eng:
                    need = max(need, self.GAP_WAR - (self.icnt[eng] - w[2]))
                for r in self.readers.get(b, ()):
                    if r[0] == "eng" and r[1] == eng:
                        need = max(need, self.GAP_WAR - (self.icnt[eng] - r[2]))
            if need > 0:
                self._spacer(eng, need)
        self.ops[eng].append(("call", fn))
        self.icnt[eng] += 1
        idx = self.icnt[eng]
        for b in reads:
            self.readers.setdefault(b, []).append(("eng", eng, idx))
        for b in writes:
            self.writer[b] = ("eng", eng, idx)
            self.readers[b] = []

    def dma(self, dst, src, role, reads=(), writes=(), queue="sp"):
        eng = queue
        self._pre(eng, reads, writes)
        self.ops[eng].append(("call", lambda E, dst=dst, src=src, s=self.dsem[role]:
                              E.dma_start(out=dst, in_=src).then_inc(s, 16)))
        self.icnt[eng] += 1
        self.dissued[role] += 1
        n = self.dissued[role]
        for b in reads:
            self.readers.setdefault(b, []).append(("dma", role, n))
        for b in writes:
            self.writer[b] = ("dma", role, n)
            self.readers[b] = []

    def alias(self, newkey, *oldkeys):
        rd = list(self.readers.get(newkey, ()))
        wr = self.writer.get(newkey)
        for ok in oldkeys:
            w = self.writer.get(ok)
            if w is not None:
                rd.append(w)
            rd.extend(self.readers.get(ok, ()))
        if wr is not None:
            rd.append(wr)
        self.readers[newkey] = rd
        self.writer.pop(newkey, None)

    def finalize(self):
        nc = self.nc
        self.ops["sp"].append(("wait", self.dsem["out"], 16 * self.dissued["out"]))
        with nc.Block() as blk:
            def mkbody(e):
                def body(E):
                    i = 0
                    for item in self.ops[e]:
                        if item[0] == "wait":
                            E.wait_ge(item[1], item[2])
                        else:
                            ins = item[1](E)
                            for so, k in self.incs[e].get(i, ()):
                                ins.then_inc(so, k)
                            i += 1
                return body
            blk.tensor(mkbody("pe"))
            blk.vector(mkbody("dve"))
            blk.scalar(mkbody("act"))
            blk.gpsimd(mkbody("pool"))
            blk.sync(mkbody("sp"))



def _colform(b):
    return np.ascontiguousarray(np.asarray(b, np.float32).reshape(-1, 128).T)


ROWOFF = {}
_off = 0
for _n, _w in (("buq", D), ("bqr", D), ("bkr", D), ("buv", D), ("bo", D),
               ("bg", 8), ("bc", 32), ("outb", D)):
    ROWOFF[_n] = (_off, _w)
    _off += _w
NROWS = _off


def host_prepare(inp):
    g = {}
    for k in ("Wd_q_w", "Wu_q_w", "Wq_r_w", "Wk_r_w", "Wd_kv_w", "Wu_k_w",
              "Wu_v_w", "Wo_w"):
        g[k] = np.ascontiguousarray(np.asarray(inp[f"L0_attn_a_{k}"], np.float32))
    g["bq_col"] = _colform(inp["L0_attn_a_Wd_q_b"])
    g["bkv_col"] = _colform(inp["L0_attn_a_Wd_kv_b"])
    g["buk_col"] = _colform(inp["L0_attn_a_Wu_k_b"])
    rows = [np.asarray(inp[f"L0_attn_a_{n}"], np.float32).reshape(-1) for n in
            ("Wu_q_b", "Wq_r_b", "Wk_r_b", "Wu_v_b", "Wo_b")]
    rows += [np.asarray(inp["L0_ffn_bg"], np.float32).reshape(-1),
             np.concatenate([np.asarray(inp["L0_attn_b_B_b"], np.float32),
                             np.asarray(inp["L0_attn_b_C_b"], np.float32)]),
             np.asarray(inp["L0_attn_b_out_b"], np.float32).reshape(-1)]
    g["rows_f32"] = np.ascontiguousarray(np.concatenate(rows).reshape(1, -1))
    g["WgT"] = np.ascontiguousarray(np.asarray(inp["L0_ffn_Wg"], np.float32).T)
    g["W1r"] = np.ascontiguousarray(
        np.asarray(inp["L0_ffn_W1"], np.float32).transpose(2, 0, 1).reshape(D, E * R))
    g["W2s"] = np.ascontiguousarray(
        np.asarray(inp["L0_ffn_W2"], np.float32).transpose(0, 2, 1).reshape(E * R, D))
    g["in_w"] = np.ascontiguousarray(np.asarray(inp["L0_attn_b_in_w"], np.float32))
    g["dt_w"] = np.ascontiguousarray(np.asarray(inp["L0_attn_b_dt_w"], np.float32))
    g["out_w"] = np.ascontiguousarray(np.asarray(inp["L0_attn_b_out_w"], np.float32))
    bcw = np.concatenate([np.asarray(inp["L0_attn_b_B_w"], np.float32),
                          np.asarray(inp["L0_attn_b_C_w"], np.float32)], axis=1)
    g["BC_w"] = np.ascontiguousarray(
        bcw.reshape(16, 128, 32).transpose(1, 0, 2).reshape(128, 512))
    g["in_b_col"] = _colform(inp["L0_attn_b_in_b"])
    cw = np.asarray(inp["L0_attn_b_conv_w"], np.float32)[:, 0, :].T
    g["conv_col"] = np.ascontiguousarray(
        cw.reshape(NDC, 128, 4).transpose(1, 0, 2).reshape(128, NDC * 4))
    g["convb_col"] = _colform(inp["L0_attn_b_conv_b"])
    g["dtb_col"] = _colform(inp["L0_attn_b_dt_b"])
    g["Dp_col"] = _colform(inp["L0_attn_b_Dp"])
    Am = np.asarray(inp["L0_attn_b_A"], np.float32)
    g["A_col"] = np.ascontiguousarray(
        Am.reshape(NDC, 128, DS).transpose(1, 0, 2).reshape(128, NDC * DS))
    pos = np.arange(S, dtype=np.float64)[:, None]
    ids = np.arange(D // 2, dtype=np.float64)
    theta = (10000.0 ** (-2.0 * ids)) / (D // 2)
    r = pos * theta[None, :]
    g["cos_t"] = np.cos(r).astype(ml_dtypes.bfloat16)
    g["sin_t"] = np.sin(r).astype(ml_dtypes.bfloat16)
    g["ident_bf"] = np.eye(128).astype(ml_dtypes.bfloat16)
    g["ones_bf"] = np.ones((128, 128), dtype=ml_dtypes.bfloat16)
    g["epsvec"] = np.full((128, 1), EPS, dtype=np.float32)
    g["onevec"] = np.full((128, 1), 1.0, dtype=np.float32)
    return g


TAP = [None]
LAST_SCH = None


def build_program(n_sub=4):
    nc = bass.Bass("TRN2", target_bir_lowering=False)
    sch = Sch(nc)

    dx = nc.dram_tensor("x", [SEQ * S, D], F32, kind="ExternalInput").ap()
    dout = nc.dram_tensor("out", [SEQ * S, D], F32, kind="ExternalOutput").ap()
    dw = {}
    wshapes = {
        "Wd_q_w": (D, DQ), "Wu_q_w": (DQ, D), "Wq_r_w": (DQ, D), "Wk_r_w": (D, D),
        "Wd_kv_w": (D, DKV), "Wu_k_w": (DKV, D), "Wu_v_w": (DKV, D), "Wo_w": (D, D),
        "bq_col": (128, 4), "bkv_col": (128, 4), "buk_col": (128, 8),
        "rows_f32": (1, NROWS), "WgT": (D, 8), "W1r": (D, E * R), "W2s": (E * R, D),
        "in_w": (D, 2 * DI), "dt_w": (DI, DI), "out_w": (DI, D), "BC_w": (128, 512),
        "in_b_col": (128, 32), "conv_col": (128, NDC * 4), "convb_col": (128, NDC),
        "dtb_col": (128, NDC), "Dp_col": (128, NDC), "A_col": (128, NDC * DS),
        "epsvec": (128, 1), "onevec": (128, 1),
    }
    for name, shape in wshapes.items():
        dw[name] = nc.dram_tensor(name, list(shape), F32, kind="ExternalInput").ap()
    for name in ("cos_t", "sin_t"):
        dw[name] = nc.dram_tensor(name, [S, D // 2], BF16, kind="ExternalInput").ap()
    for name in ("ident_bf", "ones_bf"):
        dw[name] = nc.dram_tensor(name, [128, 128], BF16, kind="ExternalInput").ap()

    scr = {k: nc.dram_tensor("scr_" + k, list(sh), BF16, kind="Internal").ap()
           for k, sh in (("in_w", (D, 2 * DI)), ("dt_w", (DI, DI)),
                         ("out_w", (DI, D)), ("BC_w", (128, 512)))}
    scr_bc = nc.dram_tensor("scr_bc", [32, TCH], BF16, kind="Internal").ap()
    scr_mh = nc.dram_tensor("scr_mh", [1, 8], F32, kind="Internal").ap()
    scr_res = nc.dram_tensor("scr_res", [S, D], BF16, kind="Internal").ap()

    AB = 16512
    CAP = 229376
    cur = [AB]

    def place(name, shape, dtype, at=None):
        nbytes = int(np.prod(shape[1:])) * (2 if dtype == BF16 else 4)
        if at is None:
            at = cur[0]
            cur[0] = at + ((nbytes + 31) // 32) * 32
        assert at + nbytes <= CAP, f"{name}: {at}+{nbytes} > {CAP}"
        return nc.alloc_sbuf_tensor_at(name, list(shape), dtype, offset=at).ap()

    res = place("res", (128, SEQ * NTCH * D), BF16)
    ident = place("ident", (128, 128), BF16)
    ones_bf = place("ones", (128, 128), BF16)
    cA = place("cA", (128, NDC * DS), F32)
    cconv = place("cconv", (128, NDC * 4), F32)
    cconvb = place("cconvb", (128, NDC), F32)
    cdtb = place("cdtb", (128, NDC), F32)
    cDp = place("cDp", (128, NDC), F32)
    cinb = place("cinb", (128, 32), F32)
    cbq = place("cbq", (128, 4), F32)
    cbkv = place("cbkv", (128, 4), F32)
    cbuk = place("cbuk", (128, 8), F32)
    ceps = place("ceps", (128, 1), F32)
    cone = place("cone", (128, 1), F32)
    rows_all = place("rows_all", (1, NROWS), BF16)
    mv = place("mv", (128, 2 * NTCH), F32)
    rstd = place("rstd", (128, NTCH), F32)
    nrm = place("nrm", (128, 64), F32)
    mhat = place("mhat", (128, 1), F32)
    gm1n = place("gm1n", (128, 1), F32)
    glg = place("glg", (128, 8), F32)
    gm1 = place("gm1", (128, 1), F32)
    gm2 = place("gm2", (128, 1), F32)
    gzz = place("gzz", (128, 1), F32)
    gzr = place("gzr", (128, 1), F32)
    gtA = place("gtA", (128, 8), F32)
    gtB = place("gtB", (128, 8), F32)
    mh_bf = place("mh_bf", (1, 1), BF16)
    lntile = place("lntile", (128, D), BF16)
    ropet = place("ropet", (128, D), BF16)
    csbuf = place("csbuf", (128, D), BF16)
    ctq_c = place("ctq_c", (128, DQ), BF16)
    rsb = place("rsb", (128, 256), F32)
    lntile2 = ropet
    FIXEND = cur[0]

    WST = cur[0]
    wstage = place("wstage", (128, 2048), F32, at=WST)
    xstage = place("xstage", (128, 2048), F32, at=WST)
    cur[0] = WST + 8192
    BIG = cur[0]
    assert CAP - BIG >= 112 * 1024, f"BIG={CAP - BIG}"

    QT = place("QT", (128, 8 * S), BF16, at=BIG)
    KT = place("KT", (128, 8 * S), BF16, at=BIG + 32768)
    xnT = place("xnT", (128, 8 * S), BF16, at=BIG + 65536)
    V65 = place("V65", (128, NTCH * H * 65), BF16, at=BIG + 65536)
    RW = BIG + ((65536 + NTCH * H * 65 * 2 + 31) // 32) * 32
    w_a = place("w_a", (128, 4096), BF16, at=RW)
    w_b = place("w_b", (128, 4096), BF16, at=RW + 8192)
    w_ab = place("w_ab", (128, 8192), BF16, at=RW)
    rs_bf = place("rs_bf", (128, 512), BF16, at=RW)
    rbb = place("rbb", (128, 512), BF16, at=RW + 1024)
    sqb = place("sqb", (128, 512), BF16, at=RW + 2048)
    rows_f32s = place("rows_f32s", (1, NROWS), F32, at=BIG + 65536)
    assert RW + 16384 <= CAP, f"attn overlay overflow: {RW + 16384 - CAP}"

    f_xnT = place("f_xnT", (128, 8 * SEQ * S), BF16, at=BIG)
    f_w1 = place("f_w1", (128, 8 * E * R), BF16, at=BIG + 65536)
    f_w2 = place("f_w2", (128, 4 * D), BF16, at=BIG + 73728)
    f_wg = place("f_wg", (128, 64), BF16, at=BIG + 81920)
    f_g = place("f_g", (128, SEQ * NTCH * E), F32, at=BIG + 82176)
    f_gh = place("f_gh", (128, E * R), BF16, at=BIG + 86272)
    f_ghT = place("f_ghT", (128, E * R), BF16, at=BIG + 87296)

    MB = BIG
    m_xnT = place("m_xnT", (128, 8 * TCH), BF16, at=MB)
    m_xz = place("m_xz", (128, NDC * (TCH + 3)), BF16, at=MB + 8192)
    m_dt = place("m_dt", (128, NDC * TCH), BF16, at=MB + 8192)
    m_xc = place("m_xc", (128, NDC * TCH), BF16, at=MB + 25088)
    m_sg = place("m_sg", (128, NDC * TCH), BF16, at=MB + 41472)
    m_v = place("m_v", (128, 4 * TCH), BF16, at=MB + 57856)
    m_ye = place("m_ye", (128, 4 * TCH), BF16, at=MB + 61952)
    m_yo = place("m_yo", (128, 4 * TCH), BF16, at=MB + 66048)
    m_bc = place("m_bc", (128, 4 * TCH), BF16, at=MB + 70144)
    m_dA = place("m_dA", (128, 4 * TCH), F32, at=MB + 96000)
    m_sp = place("m_sp", (128, 2 * TCH), F32, at=MB + 74240)
    m_h = place("m_h", (128, 4 * TCH), BF16, at=MB + 104192)
    m_dbx = place("m_dbx", (128, 4 * TCH), BF16, at=MB + 108288)
    m_carry = place("m_carry", (128, SEQ * NDC * DS), F32, at=MB + 82432)
    m_bmcm = place("m_bmcm", (128, 4 * 32), BF16, at=MB + 84480)
    m_bmT = place("m_bmT", (32, 512), BF16, at=MB + 84736)
    m_haloA = place("m_haloA", (128, SEQ * NDC * 3), BF16, at=MB + 85760)
    m_mini = place("m_mini", (128, 64), BF16, at=MB + 86144)
    m_wk = place("m_wk", (128, 4 * 1024), BF16, at=MB + 87424)
    m_bcw = place("m_bcw", (128, 512), BF16, at=MB + 86400)
    MEND = MB + 112384
    assert BIG + 112384 <= CAP, f"mamba overlay {BIG + 112384 - CAP}"

    pb = [nc.alloc_psum_tensor(f"pb{i}", [128, 512], F32).ap() for i in range(8)]
    pbh = [p.bitcast(BF16) for p in pb]

    for dst, name in ((ident, "ident_bf"), (ones_bf, "ones_bf"), (cA, "A_col"),
                      (cconv, "conv_col"), (cconvb, "convb_col"), (cdtb, "dtb_col"),
                      (cDp, "Dp_col"), (cinb, "in_b_col"), (cbq, "bq_col"),
                      (cbkv, "bkv_col"), (cbuk, "buk_col"), (ceps, "epsvec"),
                      (cone, "onevec")):
        sch.dma(dst, dw[name], "cst", writes=("consts",))
    sch.dma(rows_f32s, dw["rows_f32"], "cst", writes=("rows_f",))
    sch.op("dve", lambda v: v.tensor_copy(out=rows_all, in_=rows_f32s),
           reads=("rows_f",), writes=("consts",))

    def row(name, lo=0, hi=None):
        o, w = ROWOFF[name]
        if hi is None:
            hi = w
        return rows_all[:, o + lo:o + hi]

    for t in range(SEQ * NTCH):
        hh = t % 2
        sl = xstage[:, hh * 1024:(hh + 1) * 1024]
        sch.dma(sl, dx[t * 128:(t + 1) * 128, :], f"xin{hh}", writes=(f"xs{hh}",))
        sch.op("dve", lambda v, sl=sl, t=t: v.tensor_copy(
            out=res[:, t * D:(t + 1) * D], in_=sl),
            reads=(f"xs{hh}",), writes=(f"res{t}",))

    def cast_weights():
        for wname in ("in_w", "dt_w", "out_w", "BC_w"):
            sch.dma(scr[wname], dw[wname], "wmc", writes=(f"scr_{wname}",),
                    queue="pool")

    def ln_chunk(t, slot, outtile, okey):
        rt = res[:, t * D:(t + 1) * D]
        rk = f"res{t}"
        mcol = mv[:, slot * 2:slot * 2 + 1]
        vcol = mv[:, slot * 2 + 1:slot * 2 + 2]
        sch.op("dve", lambda v, rt=rt, mcol=mcol: v.tensor_reduce(
            out=mcol, in_=rt, axis=AX.X, op=OP.add),
            reads=(rk,), writes=(f"mv{slot}",))
        sch.op("act", lambda s_, rt=rt: s_.activation(
            out=csbuf, in_=rt, func=AF.Square),
            reads=(rk,), writes=("csb",))
        sch.op("dve", lambda v, vcol=vcol: v.tensor_reduce(
            out=vcol, in_=csbuf, axis=AX.X, op=OP.add),
            reads=("csb",), writes=(f"mv{slot}",))
        sch.op("dve", lambda v, mcol=mcol: v.tensor_scalar(
            out=mcol, in0=mcol, scalar1=1.0 / D, scalar2=None, op0=OP.mult),
            reads=(f"mv{slot}",), writes=(f"mv{slot}",))
        sch.op("dve", lambda v, vcol=vcol: v.tensor_scalar(
            out=vcol, in0=vcol, scalar1=1.0 / D, scalar2=None, op0=OP.mult),
            reads=(f"mv{slot}",), writes=(f"mv{slot}",))
        sch.op("dve", lambda v, mcol=mcol, vcol=vcol: v.scalar_tensor_tensor(
            out=vcol, in0=mcol, scalar=mcol, in1=vcol, op0=OP.mult,
            op1=OP.subtract),
            reads=(f"mv{slot}",), writes=(f"mv{slot}",))
        rs = rstd[:, slot:slot + 1]
        sch.op("act", lambda s_, vcol=vcol, rs=rs: s_.activation(
            out=rs, in_=vcol, func=AF.Ln, bias=ceps, scale=-1.0),
            reads=(f"mv{slot}", "consts"), writes=(f"rstd{slot}",))
        sch.op("act", lambda s_, rs=rs: s_.activation(
            out=rs, in_=rs, func=AF.Exp, scale=-0.5),
            reads=(f"rstd{slot}",), writes=(f"rstd{slot}",))
        sch.op("dve", lambda v, rt=rt, outtile=outtile, mcol=mcol, rs=rs:
               v.tensor_scalar(out=outtile, in0=rt, scalar1=mcol,
                               scalar2=rs, op0=OP.subtract, op1=OP.mult),
               reads=(rk, f"mv{slot}", f"rstd{slot}"), writes=(okey,))

    def transp8(intile, inkey, dstfn, dkey, bank=6, bank2=7):
        for j in range(8):
            bk = bank if j % 2 == 0 else bank2
            sl = pbh[bk][:, ((j // 2) % 2) * 128:((j // 2) % 2) * 128 + 128]
            sch.op("pe", lambda T, j=j, sl=sl, intile=intile: T.transpose(
                out=sl, in_=intile[:, j * 128:j * 128 + 128], identity=ident),
                reads=(inkey, "consts"), writes=(f"ps{bk}",))
            sch.op("dve", lambda v, j=j, sl=sl, dstfn=dstfn: v.tensor_copy(
                out=dstfn(j), in_=sl), reads=(f"ps{bk}",), writes=(dkey,))

    def load_w(dramap, dst, dstkey, role="wa"):
        K, M = dramap.shape
        for k in range(K // 128):
            for m0 in range(0, M, 1024):
                mw = min(1024, M - m0)
                half = ((k * M + m0) // 1024) % 2
                st = wstage[:, half * 1024:half * 1024 + mw]
                sk = f"wst{half}"
                sch.dma(st, dramap[k * 128:(k + 1) * 128, m0:m0 + mw],
                        role + str(half), writes=(sk,))
                sch.op("dve", lambda v, st=st, dst=dst, k=k, m0=m0, M=M, mw=mw:
                       v.tensor_copy(out=dst[:, k * M + m0:k * M + m0 + mw], in_=st),
                       reads=(sk,), writes=(dstkey,))

    def rope_psum(pbase):
        rk = (f"ps{pbase}", f"ps{pbase + 1}", "csb")
        for half in range(2):
            p = pb[pbase + half]
            c0 = csbuf[:, half * 256:half * 256 + 256]
            s0 = csbuf[:, 512 + half * 256:512 + half * 256 + 256]
            ev = ropet[:, half * 512:half * 512 + 512:2]
            od = ropet[:, half * 512 + 1:half * 512 + 512:2]
            pe_ = p[:, 0:512:2]
            po_ = p[:, 1:512:2]
            t1 = rsb[:, 0:256]
            sch.op("dve", lambda v, pe_=pe_, c0=c0, t1=t1: v.tensor_tensor(
                out=t1, in0=pe_, in1=c0, op=OP.mult), reads=rk, writes=("rt1",))
            sch.op("dve", lambda v, po_=po_, s0=s0, ev=ev: v.tensor_tensor(
                out=ev, in0=po_, in1=s0, op=OP.mult), reads=rk, writes=("ropet",))
            sch.op("dve", lambda v, ev=ev, t1=t1: v.scalar_tensor_tensor(
                out=ev, in0=ev, scalar=-1.0, in1=t1, op0=OP.mult, op1=OP.add),
                reads=("rt1", "ropet"), writes=("ropet",))
            sch.op("dve", lambda v, po_=po_, c0=c0, t1=t1: v.tensor_tensor(
                out=t1, in0=po_, in1=c0, op=OP.mult), reads=rk, writes=("rt1",))
            sch.op("dve", lambda v, pe_=pe_, s0=s0, od=od: v.tensor_tensor(
                out=od, in0=pe_, in1=s0, op=OP.mult), reads=rk, writes=("ropet",))
            sch.op("dve", lambda v, od=od, t1=t1: v.tensor_tensor(
                out=od, in0=od, in1=t1, op=OP.add),
                reads=("rt1", "ropet"), writes=("ropet",))

    def norm_bound():
        sch.op("dve", lambda v: v.memset(nrm, 0.0), writes=("nrm",))
        for src, srckey, col in ((QT, "QT", 0), (KT, "KT", 1)):
            for tb in range(4):
                for k in range(8):
                    sch.op("act", lambda s_, src=src, k=k, tb=tb: s_.activation(
                        out=sqb, in_=src[:, k * S + tb * 512:k * S + tb * 512 + 512],
                        func=AF.Square), reads=(srckey,), writes=("sqb",))
                    sch.op("pe", lambda T, k=k: T.matmul(
                        out=pb[6][0:1, :], lhsT=ones_bf[:, 0:1], rhs=sqb,
                        start=(k == 0), stop=(k == 7)),
                        reads=("sqb", "consts"), writes=("ps6",))
                sch.op("dve", lambda v, col=col: v.tensor_reduce(
                    out=nrm[0:1, 32 + col * 8:33 + col * 8], in_=pb[6][0:1, :],
                    axis=AX.X, op=OP.max), reads=("ps6",), writes=("nrm",))
                sch.op("dve", lambda v, col=col: v.tensor_tensor(
                    out=nrm[0:1, col * 8:col * 8 + 1], in0=nrm[0:1, col * 8:col * 8 + 1],
                    in1=nrm[0:1, 32 + col * 8:33 + col * 8], op=OP.max),
                    reads=("nrm",), writes=("nrm",))
        sch.op("dve", lambda v: v.tensor_tensor(
            out=nrm[0:1, 16:17], in0=nrm[0:1, 0:1], in1=nrm[0:1, 8:9], op=OP.mult),
            reads=("nrm",), writes=("nrm",))
        sch.op("act", lambda s_: s_.activation(
            out=nrm[0:1, 16:17], in_=nrm[0:1, 16:17], func=AF.Ln),
            reads=("nrm",), writes=("nrm",))
        sch.op("act", lambda s_: s_.activation(
            out=nrm[0:1, 24:25], in_=nrm[0:1, 16:17], func=AF.Exp, scale=0.5),
            reads=("nrm",), writes=("nrm",))
        sch.op("dve", lambda v: v.tensor_scalar(
            out=nrm[0:1, 24:25], in0=nrm[0:1, 24:25], scalar1=-SCALE_ATT,
            scalar2=None, op0=OP.mult), reads=("nrm",), writes=("nrm",))
        sch.dma(scr_mh, nrm[0:1, 24:32], "sc", reads=("nrm",), writes=("scr_mh",))
        sch.dma(mhat, scr_mh[0:1, 0:1].partition_broadcast(128).squeeze(1),
                "sc", reads=("scr_mh",), writes=("mhat",))

    def dump(ap, keys, nrows=128):
        N = ap.shape[1]
        for c0 in range(0, min(N, 32768), 1024):
            cw = min(1024, N - c0)
            hh = (c0 // 1024) % 2
            sl = xstage[:, hh * 1024:hh * 1024 + cw]
            sch.op("dve", lambda v, sl=sl, ap=ap, c0=c0, cw=cw: v.tensor_copy(
                out=sl, in_=ap[:, c0:c0 + cw]), reads=keys, writes=(f"os{hh}",))
            r0 = (c0 // 1024) * 128
            cw2 = max(cw, 8)
            sch.dma(dout[r0:r0 + 128, 0:cw2], xstage[:, hh * 1024:hh * 1024 + cw2],
                    "out", reads=(f"os{hh}",))

    def attention(s):
        so = 1 - s
        sch.alias("wst0", "xs0", "xs1", "fo0", "fo1")
        sch.alias("wst1", "xs0", "xs1", "fo0", "fo1")
        spill_keys = tuple(f"res{t}" for t in range(so * NTCH, (so + 1) * NTCH))
        for t in range(so * NTCH, (so + 1) * NTCH):
            tt = t - so * NTCH
            sch.dma(scr_res[tt * 128:(tt + 1) * 128, :], res[:, t * D:(t + 1) * D],
                    "sc", reads=(f"res{t}",), writes=(f"scrres{tt}",))
        rh = AB + so * 32768
        a_wqr = nc.alloc_sbuf_tensor_at(f"a_wqr{s}", [128, 4096], BF16, offset=rh).ap()
        a_ctk = nc.alloc_sbuf_tensor_at(f"a_ctk{s}", [128, 8192], BF16, offset=rh).ap()
        a_wuv = nc.alloc_sbuf_tensor_at(f"a_wuv{s}", [128, 4096], BF16,
                                        offset=rh + 16384).ap()
        a_wo = nc.alloc_sbuf_tensor_at(f"a_wo{s}", [128, 8192], BF16, offset=rh).ap()
        a_ao = nc.alloc_sbuf_tensor_at(f"a_ao{s}", [128, 4096], BF16,
                                       offset=rh + 16384).ap()
        a_pt = nc.alloc_sbuf_tensor_at(f"a_pt{s}", [128, 2048], BF16,
                                       offset=rh + 24576).ap()
        KLO, KHI, KPT = f"rhlo{s}", f"rhhi{s}", f"rhpt{s}"
        sch.alias(KLO, *spill_keys)
        sch.alias(KHI, *spill_keys)
        sch.alias(KPT, *spill_keys)

        sch.alias("xnT", "V65", "rows_f")
        sch.alias("lntile", "rsbs")
        sch.alias("lntile2", "ropet")
        for c in range(NTCH):
            lt = lntile if c % 2 == 0 else lntile2
            lk = "lntile" if c % 2 == 0 else "lntile2"
            ln_chunk(s * NTCH + c, c, lt, lk)
            transp8(lt, lk,
                    lambda j, c=c: xnT[:, j * S + c * 128:j * S + c * 128 + 128],
                    "xnT")
        if TAP[0] and TAP[0][0] == f"a0_{s}":
            dump(xnT, ("xnT",))
            return
        sch.alias("ropet", "lntile2")
        load_w(dw["Wd_q_w"], w_a, "w_a")
        load_w(dw["Wu_q_w"], w_b, "w_b")
        load_w(dw["Wq_r_w"], a_wqr, KLO)
        for c in range(NTCH):
            sch.dma(csbuf[:, 0:512], dw["cos_t"][c * 128:(c + 1) * 128, :], "cs0",
                    writes=("csb",))
            sch.dma(csbuf[:, 512:1024], dw["sin_t"][c * 128:(c + 1) * 128, :], "cs1",
                    writes=("csb",))
            for m in range(4):
                bk = 4 + m % 2
                po = pb[bk][:, 0:128]
                for k in range(8):
                    sch.op("pe", lambda T, m=m, k=k, c=c, po=po: T.matmul(
                        out=po, lhsT=w_a[:, k * DQ + m * 128:k * DQ + m * 128 + 128],
                        rhs=xnT[:, k * S + c * 128:k * S + c * 128 + 128],
                        start=(k == 0), stop=(k == 7)),
                        reads=("w_a", "xnT"), writes=(f"ps{bk}",))
                sch.op("dve", lambda v, m=m, po=po: v.tensor_scalar(
                    out=ctq_c[:, m * 128:(m + 1) * 128], in0=po,
                    scalar1=cbq[:, m:m + 1], scalar2=None, op0=OP.add),
                    reads=(f"ps{bk}", "consts"), writes=("ctq_c",))
            for half in range(2):
                for wsb, wk, rname, bank in ((w_b, "w_b", "buq", half),
                                             (a_wqr, KLO, "bqr", 2 + half)):
                    for k in range(4):
                        sch.op("pe", lambda T, k=k, half=half, wsb=wsb, bank=bank:
                               T.matmul(out=pb[bank],
                                        lhsT=ctq_c[:, k * 128:(k + 1) * 128],
                                        rhs=wsb[:, k * D + half * 512:k * D + half * 512 + 512],
                                        start=(k == 0), stop=False),
                               reads=("ctq_c", wk), writes=(f"ps{bank}",))
                    sch.op("pe", lambda T, half=half, rname=rname, bank=bank:
                           T.matmul(out=pb[bank], lhsT=ones_bf[0:1, :],
                                    rhs=row(rname, half * 512, half * 512 + 512),
                                    start=False, stop=True),
                           reads=("consts",), writes=(f"ps{bank}",))
            rope_psum(2)
            for half in range(2):
                sch.op("dve", lambda v, half=half: v.tensor_tensor(
                    out=lntile[:, half * 512:half * 512 + 512], in0=pb[half],
                    in1=ropet[:, half * 512:half * 512 + 512], op=OP.add),
                    reads=(f"ps{half}", "ropet"), writes=("lntile",))
            transp8(lntile, "lntile",
                    lambda j, c=c: QT[:, j * S + c * 128:j * S + c * 128 + 128],
                    "QT", bank=6, bank2=7)
        if TAP[0] and TAP[0][0] == f"qt_{s}":
            dump(QT, ("QT",))
            return
        sch.alias("w_ab", "w_a", "w_b")
        load_w(dw["Wk_r_w"], w_ab, "w_ab")
        for c in range(NTCH):
            sch.dma(csbuf[:, 0:512], dw["cos_t"][c * 128:(c + 1) * 128, :], "cs0",
                    writes=("csb",))
            sch.dma(csbuf[:, 512:1024], dw["sin_t"][c * 128:(c + 1) * 128, :], "cs1",
                    writes=("csb",))
            for half in range(2):
                for k in range(8):
                    sch.op("pe", lambda T, k=k, half=half, c=c: T.matmul(
                        out=pb[2 + half],
                        lhsT=xnT[:, k * S + c * 128:k * S + c * 128 + 128],
                        rhs=w_ab[:, k * D + half * 512:k * D + half * 512 + 512],
                        start=(k == 0), stop=False),
                        reads=("xnT", "w_ab"), writes=(f"ps{2 + half}",))
                sch.op("pe", lambda T, half=half: T.matmul(
                    out=pb[2 + half], lhsT=ones_bf[0:1, :],
                    rhs=row("bkr", half * 512, half * 512 + 512),
                    start=False, stop=True),
                    reads=("consts",), writes=(f"ps{2 + half}",))
            rope_psum(2)
            transp8(ropet, "ropet",
                    lambda j, c=c: KT[:, j * S + c * 128:j * S + c * 128 + 128],
                    "KT", bank=5, bank2=6)
        sch.alias(KLO, "w_ab")
        load_w(dw["Wd_kv_w"], w_a, "w_a")
        for c in range(NTCH):
            for m in range(4):
                bk = 4 + m % 2
                po = pb[bk][:, 0:128]
                for k in range(8):
                    sch.op("pe", lambda T, m=m, k=k, c=c, po=po: T.matmul(
                        out=po, lhsT=w_a[:, k * DKV + m * 128:k * DKV + m * 128 + 128],
                        rhs=xnT[:, k * S + c * 128:k * S + c * 128 + 128],
                        start=(k == 0), stop=(k == 7)),
                        reads=("w_a", "xnT"), writes=(f"ps{bk}",))
                sch.op("dve", lambda v, m=m, c=c, po=po: v.tensor_scalar(
                    out=a_ctk[:, m * S + c * 128:m * S + c * 128 + 128], in0=po,
                    scalar1=cbkv[:, m:m + 1], scalar2=None, op0=OP.add),
                    reads=(f"ps{bk}", "consts"), writes=(KLO,))
        load_w(dw["Wu_k_w"], w_b, "w_b")
        for m in range(8):
            for tb in range(4):
                bank = tb % 2
                for k in range(4):
                    sch.op("pe", lambda T, m=m, tb=tb, k=k, bank=bank: T.matmul(
                        out=pb[bank],
                        lhsT=w_b[:, k * D + m * 128:k * D + m * 128 + 128],
                        rhs=a_ctk[:, k * S + tb * 512:k * S + tb * 512 + 512],
                        start=(k == 0), stop=(k == 3)),
                        reads=("w_b", KLO), writes=(f"ps{bank}",))
                sch.op("dve", lambda v, m=m, tb=tb, bank=bank: v.scalar_tensor_tensor(
                    out=KT[:, m * S + tb * 512:m * S + tb * 512 + 512],
                    in0=pb[bank], scalar=cbuk[:, m:m + 1],
                    in1=KT[:, m * S + tb * 512:m * S + tb * 512 + 512],
                    op0=OP.add, op1=OP.add),
                    reads=(f"ps{bank}", "KT", "consts"), writes=("KT",))
        if TAP[0] and TAP[0][0] == f"kt_{s}":
            dump(KT, ("KT",))
            return
        load_w(dw["Wu_v_w"], a_wuv, KHI)
        sch.alias("V65", "xnT")
        for c in range(NTCH):
            for half in range(2):
                for k in range(4):
                    sch.op("pe", lambda T, k=k, half=half, c=c: T.matmul(
                        out=pb[2 + half],
                        lhsT=a_ctk[:, k * S + c * 128:k * S + c * 128 + 128],
                        rhs=a_wuv[:, k * D + half * 512:k * D + half * 512 + 512],
                        start=(k == 0), stop=False),
                        reads=(KLO, KHI), writes=(f"ps{2 + half}",))
                sch.op("pe", lambda T, half=half: T.matmul(
                    out=pb[2 + half], lhsT=ones_bf[0:1, :],
                    rhs=row("buv", half * 512, half * 512 + 512),
                    start=False, stop=True),
                    reads=("consts",), writes=(f"ps{2 + half}",))
            base = c * H * 65
            for h in range(H):
                bank = 2 + h // 8
                sch.op("dve", lambda v, h=h, bank=bank, base=base: v.tensor_copy(
                    out=V65[:, base + h * 65:base + h * 65 + 64],
                    in_=pb[bank][:, (h % 8) * 64:(h % 8) * 64 + 64]),
                    reads=(f"ps{bank}",), writes=("V65",))
            sch.op("dve", lambda v, base=base: v.memset(
                V65[:, base + 64:base + H * 65:65], 1.0), writes=("V65",))
        for kk in ("sqb", "rs_bf", "rbb"):
            sch.alias(kk, "w_a", "w_b", "w_ab", "wst0", "wst1")
        if TAP[0] and TAP[0][0] == f"v_{s}":
            dump(V65, ("V65",))
            return
        norm_bound()
        if TAP[0] and TAP[0][0] == f"nb_{s}":
            dump(nrm, ("nrm", "mhat"))
            return
        if TAP[0] and TAP[0][0] == f"mh_{s}":
            dump(mhat, ("mhat",))
            return
        sch.alias("rsbs", "lntile")
        sch.alias(KLO, KLO)
        load_w(dw["Wo_w"], a_wo, KLO)
        sch.alias(KHI, KHI)
        for qb in range(4):
            core_qblock(s, qb, a_wo, a_ao, a_pt, KLO, KHI, KPT)
            if TAP[0] and TAP[0][0] == f"core{qb}_{s}":
                dump(a_ao, (KHI,))
                return
            if TAP[0] and TAP[0][0] == f"pt{qb}_{s}":
                dump(a_pt, ("PT0", "PT1"))
                return
        for t in range(so * NTCH, (so + 1) * NTCH):
            tt = t - so * NTCH
            sch.dma(res[:, t * D:(t + 1) * D], scr_res[tt * 128:(tt + 1) * 128, :],
                    "sc2", reads=(f"scrres{tt}",), writes=(f"res{t}", KLO, KHI, KPT))

    def core_qblock(s, qb, a_wo, a_ao, a_pt, KLO, KHI, KPT):
        def emit_S(hp, kc):
            bsel = kc % 2
            for sub, hh in enumerate((2 * hp, 2 * hp + 1)):
                stile = pb[2 * sub + bsel]
                skey = f"ps{2 * sub + bsel}"
                base = (hh % 2) * 64
                fc = hp
                sch.op("pe", lambda T, kc=kc, qb=qb, stile=stile, fc=fc,
                       base=base: T.matmul(
                    out=stile,
                    lhsT=KT[base:base + 64, fc * S + kc * 128:fc * S + kc * 128 + 128],
                    rhs=QT[base:base + 64, fc * S + qb * 512:fc * S + qb * 512 + 512],
                    start=True, stop=True, tile_position=(base, 0)),
                    reads=("KT", "QT"), writes=(skey,))

        def emit_exp(hp, kc):
            bsel = kc % 2
            for sub in range(2):
                stile = pb[2 * sub + bsel]
                skey = f"ps{2 * sub + bsel}"
                pt = a_pt[:, (2 * sub + bsel) * 512:(2 * sub + bsel) * 512 + 512]
                ptk = f"PT{2 * sub + bsel}"
                sch.op("act", lambda s_, stile=stile, pt=pt: s_.activation(
                    out=pt, in_=stile, func=AF.Exp, bias=mhat, scale=SCALE_ATT),
                    reads=(skey, "mhat"), writes=(ptk,))

        def emit_AV(hp, kc):
            bsel = kc % 2
            for sub, hh in enumerate((2 * hp, 2 * hp + 1)):
                pt = a_pt[:, (2 * sub + bsel) * 512:(2 * sub + bsel) * 512 + 512]
                ptk = f"PT{2 * sub + bsel}"
                ob = pb[4 + sub]
                sch.op("pe", lambda T, hh=hh, kc=kc, ob=ob, pt=pt: T.matmul(
                    out=ob[0:65, :],
                    lhsT=V65[:, kc * H * 65 + hh * 65:kc * H * 65 + hh * 65 + 65],
                    rhs=pt, start=(kc == 0), stop=(kc == NTCH - 1)),
                    reads=("V65", ptk), writes=(f"ps{4 + sub}",))

        for hp in range(8):
            emit_S(hp, 0)
            for kc in range(NTCH):
                emit_exp(hp, kc)
                if kc + 1 < NTCH:
                    emit_S(hp, kc + 1)
                emit_AV(hp, kc)
            for sub, hh in enumerate((2 * hp, 2 * hp + 1)):
                ob = pb[4 + sub]
                okey = f"ps{4 + sub}"
                lnf = lntile.bitcast(F32)
                sch.op("act", lambda s_, ob=ob, lnf=lnf: s_.activation(
                    out=lnf[64:65, 0:512], in_=ob[64:65, :], func=AF.Ln),
                    reads=(okey,), writes=("rsbs",))
                sch.op("act", lambda s_, lnf=lnf: s_.activation(
                    out=rs_bf[64:65, :], in_=lnf[64:65, 0:512], func=AF.Exp, scale=-1.0),
                    reads=("rsbs",), writes=("rs_bf",))
                sch.op("pe", lambda T: T.matmul(
                    out=pb[6][0:64, :], lhsT=ones_bf[64:65, 0:64],
                    rhs=rs_bf[64:65, :], start=True, stop=True,
                    tile_position=(64, 0)),
                    reads=("rs_bf", "consts"), writes=("ps6",))
                sch.op("dve", lambda v: v.tensor_copy(out=rbb[0:64, :],
                                                      in_=pb[6][0:64, :]),
                       reads=("ps6",), writes=("rbb",))
                base = (hh % 2) * 64
                fc = hp
                sch.op("dve", lambda v, ob=ob, base=base, fc=fc: v.tensor_tensor(
                    out=a_ao[base:base + 64, fc * 512:fc * 512 + 512],
                    in0=ob[0:64, :], in1=rbb[0:64, :], op=OP.mult),
                    reads=(okey, "rbb"), writes=(KHI,))
        for tc in range(4):
            t = s * NTCH + qb * 4 + tc
            for half in range(2):
                bank = 6 + half
                for k in range(8):
                    sch.op("pe", lambda T, k=k, tc=tc, half=half, bank=bank: T.matmul(
                        out=pb[bank],
                        lhsT=a_ao[:, k * 512 + tc * 128:k * 512 + tc * 128 + 128],
                        rhs=a_wo[:, k * D + half * 512:k * D + half * 512 + 512],
                        start=(k == 0), stop=False),
                        reads=(KHI, KLO), writes=(f"ps{bank}",))
                sch.op("pe", lambda T, half=half, bank=bank: T.matmul(
                    out=pb[bank], lhsT=ones_bf[0:1, :],
                    rhs=row("bo", half * 512, half * 512 + 512),
                    start=False, stop=True), reads=("consts",), writes=(f"ps{bank}",))
                sch.op("dve", lambda v, t=t, half=half, bank=bank: v.tensor_tensor(
                    out=res[:, t * D + half * 512:t * D + half * 512 + 512],
                    in0=pb[bank],
                    in1=res[:, t * D + half * 512:t * D + half * 512 + 512],
                    op=OP.add), reads=(f"ps{bank}", f"res{t}"), writes=(f"res{t}",))

    def gate_ops(t):
        gsl = f_g[:, t * 8:(t + 1) * 8]
        lg = glg
        m1 = gm1
        m1n = gm1n
        m2 = gm2
        zz = gzz
        zr = gzr
        tA = gtA
        tB = gtB
        sch.op("dve", lambda v: v.tensor_copy(out=lg, in_=pb[6][:, 0:8]),
               reads=("ps6",), writes=("glg",))
        sch.op("dve", lambda v: v.tensor_reduce(
            out=m1, in_=lg, axis=AX.X, op=OP.max), reads=("glg",), writes=("gm1",))
        sch.op("dve", lambda v: v.tensor_scalar(
            out=tA, in0=lg, scalar1=m1, scalar2=-1e30, op0=OP.is_equal,
            op1=OP.mult), reads=("glg", "gm1"), writes=("gtA",))
        sch.op("dve", lambda v: v.tensor_tensor(
            out=tB, in0=tA, in1=lg, op=OP.add), reads=("gtA", "glg"),
            writes=("gtB",))
        sch.op("dve", lambda v: v.tensor_reduce(
            out=m2, in_=tB, axis=AX.X, op=OP.max), reads=("gtB",), writes=("gm2",))
        sch.op("dve", lambda v: v.tensor_scalar(
            out=m1n, in0=m1, scalar1=-1.0, scalar2=None, op0=OP.mult),
            reads=("gm1",), writes=("gm1",))
        sch.op("act", lambda s_: s_.activation(
            out=tA, in_=lg, func=AF.Exp, bias=m1n),
            reads=("glg", "gm1", "gtA"), writes=("gtA",))
        sch.op("dve", lambda v: v.scalar_tensor_tensor(
            out=tB, in0=lg, scalar=m2, in1=tA, op0=OP.is_ge, op1=OP.mult),
            reads=("glg", "gm2", "gtA"), writes=("gtB",))
        sch.op("dve", lambda v: v.tensor_reduce(
            out=zz, in_=tB, axis=AX.X, op=OP.add), reads=("gtB",), writes=("gz",))
        sch.op("act", lambda s_: s_.activation(out=zr, in_=zz, func=AF.Ln),
               reads=("gz",), writes=("gz",))
        sch.op("act", lambda s_: s_.activation(out=zr, in_=zr, func=AF.Exp,
                                               scale=-1.0),
               reads=("gz",), writes=("gz",))
        dbg = TAP[0][1] if TAP[0] else None
        if dbg == "lg":
            sch.op("dve", lambda v, gsl=gsl: v.tensor_copy(out=gsl, in_=lg),
                   reads=("glg",), writes=("f_gk",))
        elif dbg == "stats":
            sch.op("dve", lambda v, gsl=gsl: v.tensor_copy(out=gsl[:, 0:1], in_=m1),
                   reads=("gm1",), writes=("f_gk",))
            sch.op("dve", lambda v, gsl=gsl: v.tensor_copy(out=gsl[:, 1:2], in_=m2),
                   reads=("gm2",), writes=("f_gk",))
            sch.op("dve", lambda v, gsl=gsl: v.tensor_copy(out=gsl[:, 2:3], in_=zz),
                   reads=("gz",), writes=("f_gk",))
            sch.op("dve", lambda v, gsl=gsl: v.tensor_copy(out=gsl[:, 3:4], in_=zr),
                   reads=("gz",), writes=("f_gk",))
            sch.op("dve", lambda v, gsl=gsl: v.tensor_copy(out=gsl[:, 4:8], in_=tB[:, 0:4]),
                   reads=("gtB",), writes=("f_gk",))
        else:
            sch.op("dve", lambda v, gsl=gsl: v.tensor_scalar(
                out=gsl, in0=tB, scalar1=zr, scalar2=None, op0=OP.mult),
                reads=("gtB", "gz"), writes=("f_gk",))

    def ffn(final):
        NT = SEQ * NTCH
        mkeys = ("m_xnT", "m_xz", "m_dt", "m_xc", "m_sg", "m_v", "m_ye", "m_yo",
                 "m_bcB0", "m_bcB1", "m_bcC0", "m_bcC1", "m_dA0", "m_dA1",
                 "m_dA2", "m_dA3", "m_dAi0", "m_dAi1", "m_dAi2", "m_dAi3",
                 "m_sp0", "m_sp1", "m_sp2", "m_cacc", "m_h0", "m_h1", "m_h2",
                 "m_h3", "m_dbx0", "m_dbx1", "m_dbx2", "m_dbx3",
                 "m_carry0", "m_carry1", "m_carry2", "m_carry3", "m_bmcm", "m_bmT",
                 "m_halo", "m_mini", "m_wk0", "m_wk1", "m_wk2", "m_wk3", "m_bcw")
        sch.alias("f_xnT", "QT", "KT", "xnT", "V65", *mkeys)
        for fk in ("f_w1", "f_w2", "f_wg", "f_g", "f_gh", "f_ghT", "f_gk"):
            sch.alias(fk, "xnT", "V65", "rs_bf", "rbb", "sqb", *mkeys)
        sch.alias("geh", "lntile")
        sch.alias("lntile", "rsbs")
        sch.alias("wst0", "fo0_0", "fo0_1", "fo1_0", "fo1_1")
        sch.alias("wst1", "fo0_0", "fo0_1", "fo1_0", "fo1_1")
        load_w(dw["W1r"], f_w1, "f_w1", role="wf")
        load_w(dw["W2s"], f_w2, "f_w2", role="wf")
        load_w(dw["WgT"], f_wg, "f_wg", role="wf")
        for t in range(NT):
            ln_chunk(t, t % NTCH, lntile, "lntile")
            transp8(lntile, "lntile",
                    lambda j, t=t: f_xnT[:, j * NT * 128 + t * 128:
                                         j * NT * 128 + t * 128 + 128], "f_xnT",
                    bank=4, bank2=5)
            for k in range(8):
                sch.op("pe", lambda T, k=k, t=t: T.matmul(
                    out=pb[6][:, 0:8],
                    lhsT=f_xnT[:, k * NT * 128 + t * 128:k * NT * 128 + t * 128 + 128],
                    rhs=f_wg[:, k * 8:k * 8 + 8], start=(k == 0), stop=(k == 7)),
                    reads=("f_xnT", "f_wg"), writes=("ps6",))
            sch.op("pe", lambda T: T.matmul(
                out=pb[6][:, 0:8], lhsT=ones_bf[0:1, :], rhs=row("bg"),
                start=False, stop=True), reads=("consts",), writes=("ps6",))
            gate_ops(t)
        if TAP[0] and TAP[0][0] == "gates":
            dump(f_g, ("f_gk",))
            return
        for t in range(NT):
            gsl = f_g[:, t * 8:(t + 1) * 8]
            ph = t % 2
            hb = 0 if ph == 0 else 4
            tb_ = 1 if ph == 0 else 5
            ge = lntile if ph == 0 else lntile2
            gek = "geh" if ph == 0 else "geh2"
            ghk = "f_gh" if ph == 0 else "f_gh2"
            ghTk = "f_ghT" if ph == 0 else "f_ghT2"
            gh = f_gh[:, 0:256] if False else (f_gh if ph == 0 else f_ghT)
            for k in range(8):
                sch.op("pe", lambda T, k=k, t=t, hb=hb: T.matmul(
                    out=pb[hb],
                    lhsT=f_xnT[:, k * NT * 128 + t * 128:k * NT * 128 + t * 128 + 128],
                    rhs=f_w1[:, k * E * R:(k + 1) * E * R],
                    start=(k == 0), stop=(k == 7)),
                    reads=("f_xnT", "f_w1"), writes=(f"ps{hb}",))
            sch.op("act", lambda s_, hb=hb, ge=ge: s_.activation(
                out=ge[:, 0:512], in_=pb[hb], func=AF.Gelu),
                reads=(f"ps{hb}",), writes=(gek,))
            for e in range(E):
                sch.op("dve", lambda v, e=e, gsl=gsl, ge=ge, gh=gh: v.tensor_scalar(
                    out=gh[:, e * R:(e + 1) * R], in0=ge[:, e * R:(e + 1) * R],
                    scalar1=gsl[:, e:e + 1], scalar2=None, op0=OP.mult),
                    reads=(gek, "f_gk"), writes=(ghk,))
            for j in range(4):
                sl = pbh[tb_][:, j * 128:j * 128 + 128]
                sch.op("pe", lambda T, j=j, sl=sl, gh=gh: T.transpose(
                    out=sl, in_=gh[:, j * 128:(j + 1) * 128], identity=ident),
                    reads=(ghk, "consts"), writes=(f"ps{tb_}",))
                sch.op("dve", lambda v, j=j, sl=sl, gh=gh, ph=ph: v.tensor_copy(
                    out=(f_gh if ph == 1 else f_ghT)[:, j * 128:(j + 1) * 128], in_=sl),
                    reads=(f"ps{tb_}",), writes=(ghTk,))
            for half in range(2):
                bank = (2 + half) if ph == 0 else (6 + half)
                ght = f_ghT if ph == 0 else f_gh
                for k in range(4):
                    sch.op("pe", lambda T, k=k, half=half, bank=bank, ght=ght: T.matmul(
                        out=pb[bank], lhsT=ght[:, k * 128:(k + 1) * 128],
                        rhs=f_w2[:, k * D + half * 512:k * D + half * 512 + 512],
                        start=(k == 0), stop=(k == 3)),
                        reads=(ghTk, "f_w2"), writes=(f"ps{bank}",))
                if not final:
                    sch.op("dve", lambda v, t=t, half=half, bank=bank:
                           v.scalar_tensor_tensor(
                               out=res[:, t * D + half * 512:t * D + half * 512 + 512],
                               in0=pb[bank], scalar=2.0,
                               in1=res[:, t * D + half * 512:t * D + half * 512 + 512],
                               op0=OP.mult, op1=OP.add),
                           reads=(f"ps{bank}", f"res{t}"), writes=(f"res{t}",))
                else:
                    sch.op("dve", lambda v, t=t, half=half, bank=bank, ph=ph:
                           v.scalar_tensor_tensor(
                               out=xstage[:, ph * 1024 + half * 512:
                                          ph * 1024 + (half + 1) * 512],
                               in0=pb[bank], scalar=2.0,
                               in1=res[:, t * D + half * 512:t * D + half * 512 + 512],
                               op0=OP.mult, op1=OP.add),
                           reads=(f"ps{bank}", f"res{t}"), writes=(f"fo{t % 2}_{half}",))
            if final:
                ph = t % 2
                sch.dma(dout[t * 128:(t + 1) * 128, :],
                        xstage[:, ph * 1024:(ph + 1) * 1024], "out",
                        reads=(f"fo{ph}_0", f"fo{ph}_1"))

    def softplus_evac(m, bank):
        x = pb[bank]
        t0 = m_sp[:, 0:TCH]
        t1 = m_sp[:, TCH:2 * TCH]
        dst = m_dt[:, m * TCH:(m + 1) * TCH]
        sch.op("dve", lambda v, x=x, m=m, t0=t0: v.tensor_scalar(
            out=t0, in0=x, scalar1=cdtb[:, m:m + 1], scalar2=None, op0=OP.add),
            reads=(f"ps{bank}", "consts"), writes=("m_sp0",))
        sch.op("act", lambda s_, t0=t0, t1=t1: s_.activation(
            out=t1, in_=t0, func=AF.Abs),
            reads=("m_sp0",), writes=("m_sp1",))
        sch.op("act", lambda s_, t1=t1: s_.activation(
            out=t1, in_=t1, func=AF.Exp, scale=-1.0),
            reads=("m_sp1",), writes=("m_sp1",))
        sch.op("act", lambda s_, t1=t1: s_.activation(
            out=t1, in_=t1, func=AF.Ln, bias=cone, scale=1.0),
            reads=("m_sp1", "consts"), writes=("m_sp1",))
        sch.op("dve", lambda v, t0=t0: v.tensor_scalar(
            out=t0, in0=t0, scalar1=0.0, scalar2=None, op0=OP.max),
            reads=("m_sp0",), writes=("m_sp0",))
        sch.op("dve", lambda v, t0=t0, t1=t1, dst=dst: v.tensor_tensor(
            out=dst, in0=t0, in1=t1, op=OP.add),
            reads=("m_sp0", "m_sp1"), writes=("m_dt",))

    def mamba_chunk(s, ci):
        last = (ci == NMCH - 1)
        c0 = ci * 4
        m_halo = m_haloA[:, s * NDC * 3:(s + 1) * NDC * 3]
        sch.alias("m_xz", "m_dt")
        sch.alias("m_cacc", "m_sp1", "m_sp2", "m_dAi0", "m_dAi1", "m_dAi2", "m_dAi3")
        for cc in range(4):
            t = s * NTCH + c0 + cc
            ln_chunk(t, cc, lntile, "lntile")
            transp8(lntile, "lntile",
                    lambda j, cc=cc: m_xnT[:, j * TCH + cc * 128:j * TCH + cc * 128 + 128],
                    "m_xnT")
        if not last:
            t = s * NTCH + c0 + 4
            ln_chunk(t, 4, lntile, "lntile")
            for j in range(8):
                sl = pbh[7][:, (j % 4) * 128:(j % 4) * 128 + 128]
                sch.op("pe", lambda T, j=j, sl=sl: T.transpose(
                    out=sl, in_=lntile[:, j * 128:j * 128 + 128], identity=ident),
                    reads=("lntile", "consts"), writes=("ps7",))
                sch.op("dve", lambda v, j=j, sl=sl: v.tensor_copy(
                    out=m_mini[:, j * 2:j * 2 + 2], in_=sl[:, 0:2]),
                    reads=("ps7",), writes=("m_mini",))
        for mg in range(4):
            for k in range(8):
                wb = m_wk[:, (k % 4) * 1024:(k % 4) * 1024 + 1024]
                wk = f"m_wk{k % 4}"
                sch.dma(wb, scr["in_w"][k * 128:(k + 1) * 128,
                                        mg * 1024:(mg + 1) * 1024], f"wm{k % 4}",
                        reads=("scr_in_w",), writes=(wk,))
                for m in range(8):
                    sch.op("pe", lambda T, k=k, m=m, wb=wb: T.matmul(
                        out=pb[m], lhsT=wb[:, m * 128:(m + 1) * 128],
                        rhs=m_xnT[:, k * TCH:(k + 1) * TCH],
                        start=(k == 0), stop=(k == 7)),
                        reads=(wk, "m_xnT"), writes=(f"ps{m}",))
            for m in range(8):
                gm = mg * 8 + m
                if gm < 16:
                    sch.op("dve", lambda v, gm=gm, m=m: v.tensor_scalar(
                        out=m_xz[:, gm * (TCH + 3) + 1:gm * (TCH + 3) + 1 + TCH],
                        in0=pb[m], scalar1=cinb[:, gm:gm + 1], scalar2=None,
                        op0=OP.add), reads=(f"ps{m}", "consts"), writes=("m_xz",))
                else:
                    tg_ = m_sp[:, 0:TCH]
                    sch.op("dve", lambda v, gm=gm, m=m, tg_=tg_: v.tensor_scalar(
                        out=tg_, in0=pb[m], scalar1=cinb[:, gm:gm + 1],
                        scalar2=None, op0=OP.add),
                        reads=(f"ps{m}", "consts"), writes=("m_sp0",))
                    sch.op("act", lambda s_, gm=gm, tg_=tg_: s_.activation(
                        out=m_sg[:, (gm - 16) * TCH:(gm - 15) * TCH], in_=tg_,
                        func=AF.Silu),
                        reads=("m_sp0",), writes=("m_sg",))
            if not last and mg < 2:
                for k in range(8):
                    wb = m_wk[:, (k % 2) * 1024:(k % 2) * 1024 + 1024]
                    wk = f"m_wk{k % 2}"
                    pass
        if not last:
            for k in range(8):
                wb = m_wk[:, (k % 4) * 1024:(k % 4) * 1024 + 1024]
                wk = f"m_wk{k % 4}"
                sch.dma(wb, scr["in_w"][k * 128:(k + 1) * 128, 0:1024],
                        f"wm{k % 4}", reads=("scr_in_w",), writes=(wk,))
                for m in range(8):
                    sch.op("pe", lambda T, k=k, m=m, wb=wb: T.matmul(
                        out=pb[m][:, 0:2],
                        lhsT=wb[:, m * 128:(m + 1) * 128],
                        rhs=m_mini[:, k * 2:k * 2 + 2],
                        start=(k == 0), stop=(k == 7)),
                        reads=(wk, "m_mini"), writes=(f"ps{m}",))
            for m in range(8):
                sch.op("dve", lambda v, m=m: v.tensor_scalar(
                    out=m_halo[:, m * 3 + 1:m * 3 + 3], in0=pb[m][:, 0:2],
                    scalar1=cinb[:, m:m + 1], scalar2=None, op0=OP.add),
                    reads=(f"ps{m}", "consts"), writes=("m_halo",))
            for k in range(8):
                wb = m_wk[:, (k % 4) * 1024:(k % 4) * 1024 + 1024]
                wk = f"m_wk{k % 4}"
                sch.dma(wb, scr["in_w"][k * 128:(k + 1) * 128, 1024:2048],
                        f"wm{k % 4}", reads=("scr_in_w",), writes=(wk,))
                for m in range(8):
                    sch.op("pe", lambda T, k=k, m=m, wb=wb: T.matmul(
                        out=pb[m][:, 0:2],
                        lhsT=wb[:, m * 128:(m + 1) * 128],
                        rhs=m_mini[:, k * 2:k * 2 + 2],
                        start=(k == 0), stop=(k == 7)),
                        reads=(wk, "m_mini"), writes=(f"ps{m}",))
            for m in range(8):
                mm = m + 8
                sch.op("dve", lambda v, m=m, mm=mm: v.tensor_scalar(
                    out=m_halo[:, mm * 3 + 1:mm * 3 + 3], in0=pb[m][:, 0:2],
                    scalar1=cinb[:, mm:mm + 1], scalar2=None, op0=OP.add),
                    reads=(f"ps{m}", "consts"), writes=("m_halo",))
        else:
            sch.op("dve", lambda v: v.memset(m_halo[:, 1:NDC * 3:3], 0.0),
                   writes=("m_halo",))
            sch.op("dve", lambda v: v.memset(m_halo[:, 2:NDC * 3:3], 0.0),
                   writes=("m_halo",))
        if ci == 0:
            sch.op("dve", lambda v: v.memset(m_halo[:, 0:NDC * 3:3], 0.0),
                   writes=("m_halo",))
        for m in range(NDC):
            sch.op("dve", lambda v, m=m: v.tensor_copy(
                out=m_xz[:, m * (TCH + 3):m * (TCH + 3) + 1],
                in_=m_halo[:, m * 3:m * 3 + 1]),
                reads=("m_halo",), writes=("m_xz",))
        for m in range(NDC):
            sch.op("dve", lambda v, m=m: v.tensor_copy(
                out=m_xz[:, m * (TCH + 3) + TCH + 1:m * (TCH + 3) + TCH + 3],
                in_=m_halo[:, m * 3 + 1:m * 3 + 3]),
                reads=("m_halo",), writes=("m_xz",))
        for dc in range(NDC):
            base = dc * (TCH + 3)
            full = m_xz[:, base:base + TCH + 3]
            w = lambda i, dc=dc: cconv[:, dc * 4 + i:dc * 4 + i + 1]
            acc = m_sp[:, 0:TCH]
            sch.op("dve", lambda v, full=full, w=w, acc=acc: v.tensor_scalar(
                out=acc, in0=full[:, 0:TCH], scalar1=w(0), scalar2=None,
                op0=OP.mult), reads=("m_xz", "consts"), writes=("m_cacc",))
            for i in range(1, 4):
                sch.op("dve", lambda v, full=full, w=w, acc=acc, i=i:
                       v.scalar_tensor_tensor(out=acc, in0=full[:, i:i + TCH],
                                              scalar=w(i), in1=acc,
                                              op0=OP.mult, op1=OP.add),
                       reads=("m_xz", "consts", "m_cacc"), writes=("m_cacc",))
            sch.op("dve", lambda v, dc=dc, acc=acc: v.tensor_scalar(
                out=acc, in0=acc, scalar1=cconvb[:, dc:dc + 1], scalar2=None,
                op0=OP.add),
                reads=("m_cacc", "consts"), writes=("m_cacc",))
            sch.op("act", lambda s_, dc=dc, acc=acc: s_.activation(
                out=m_xc[:, dc * TCH:(dc + 1) * TCH], in_=acc, func=AF.Silu),
                reads=("m_cacc",), writes=("m_xc",))
        if not last:
            for m in range(NDC):
                sch.op("dve", lambda v, m=m: v.tensor_copy(
                    out=m_halo[:, m * 3:m * 3 + 1],
                    in_=m_xz[:, m * (TCH + 3) + TCH:m * (TCH + 3) + TCH + 1]),
                    reads=("m_xz",), writes=("m_halo",))
        sch.dma(m_bcw, scr["BC_w"], "wmb",
                reads=("scr_BC_w",), writes=("m_bcw",))
        for tsub in range(4):
            for k in range(NDC):
                sch.op("pe", lambda T, tsub=tsub, k=k: T.matmul(
                    out=pb[3][:, 0:32],
                    lhsT=m_xc[:, k * TCH + tsub * 128:k * TCH + tsub * 128 + 128],
                    rhs=m_bcw[:, k * 32:(k + 1) * 32],
                    start=(k == 0), stop=(k == NDC - 1)),
                    reads=("m_bcw", "m_xc"), writes=("ps3",))
            sch.op("pe", lambda T: T.matmul(
                out=pb[3][:, 0:32], lhsT=ones_bf[0:1, :], rhs=row("bc"),
                start=False, stop=True), reads=("consts",), writes=("ps3",))
            sch.op("dve", lambda v, tsub=tsub: v.tensor_copy(
                out=m_bmcm[:, tsub * 32:(tsub + 1) * 32], in_=pb[3][:, 0:32]),
                reads=("ps3",), writes=("m_bmcm",))
            sch.op("pe", lambda T, tsub=tsub: T.transpose(
                out=pbh[2][0:32, 0:128],
                in_=m_bmcm[:, tsub * 32:(tsub + 1) * 32], identity=ident),
                reads=("m_bmcm", "consts"), writes=("ps2",))
            sch.op("dve", lambda v, tsub=tsub: v.tensor_copy(
                out=m_bmT[:, tsub * 128:(tsub + 1) * 128], in_=pbh[2][0:32, 0:128]),
                reads=("ps2",), writes=("m_bmT",))
        sch.dma(scr_bc, m_bmT, "bcw", reads=("m_bmT",), writes=("scr_bck",))
        sch.alias("m_dt", "m_xz")
        sch.alias("m_sp1", "m_cacc", "m_dAi0", "m_dAi1", "m_dAi2", "m_dAi3")
        sch.alias("m_sp2", "m_cacc", "m_dAi0", "m_dAi1", "m_dAi2", "m_dAi3")
        sch.alias("m_sp0", "m_cacc", "m_dAi0", "m_dAi1", "m_dAi2", "m_dAi3")
        for mg in range(2):
            for k in range(NDC):
                wb = m_wk[:, (k % 4) * 1024:(k % 4) * 1024 + 1024]
                wk = f"m_wk{k % 4}"
                sch.dma(wb, scr["dt_w"][k * 128:(k + 1) * 128,
                                        mg * 1024:(mg + 1) * 1024], f"wm{k % 4}",
                        reads=("scr_dt_w",), writes=(wk,))
                for m in range(8):
                    sch.op("pe", lambda T, k=k, m=m, wb=wb: T.matmul(
                        out=pb[m], lhsT=wb[:, m * 128:(m + 1) * 128],
                        rhs=m_xc[:, k * TCH:(k + 1) * TCH],
                        start=(k == 0), stop=(k == NDC - 1)),
                        reads=(wk, "m_xc"), writes=(f"ps{m}",))
            for m in range(8):
                softplus_evac(mg * 8 + m, m)
        scan(s)
        out_proj(s, ci)

    def scan(s):
        dAi = m_sp.bitcast(BF16)

        def bc_fetch(si):
            sch.dma(m_bc[:, (si % 2) * 2 * TCH:(si % 2) * 2 * TCH + TCH],
                    scr_bc[si:si + 1, :].partition_broadcast(128).squeeze(1),
                    f"bc{si % 2}", reads=("scr_bck",), writes=(f"m_bcB{si % 2}",))
            sch.dma(m_bc[:, (si % 2) * 2 * TCH + TCH:(si % 2) * 2 * TCH + 2 * TCH],
                    scr_bc[16 + si:17 + si, :].partition_broadcast(128).squeeze(1),
                    f"bc{si % 2}", reads=("scr_bck",), writes=(f"m_bcC{si % 2}",))

        def emit_dA(dcb, si):
            for j in range(4):
                dc = dcb * 4 + j
                ai = dAi[:, j * TCH:(j + 1) * TCH]
                sch.op("dve", lambda v, dc=dc, si=si, ai=ai: v.tensor_scalar(
                    out=ai, in0=m_dt[:, dc * TCH:(dc + 1) * TCH],
                    scalar1=cA[:, dc * DS + si:dc * DS + si + 1],
                    scalar2=None, op0=OP.mult),
                    reads=("m_dt", "consts"), writes=(f"m_dAi{j}",))
            sch.op("act", lambda s_: s_.activation(
                out=m_dA[:, 0:4 * TCH], in_=dAi[:, 0:4 * TCH], func=AF.Exp),
                reads=("m_dAi0", "m_dAi1", "m_dAi2", "m_dAi3"),
                writes=("m_dA0", "m_dA1", "m_dA2", "m_dA3"))

        for dcb in range(4):
            sch.alias("m_dAi0", "m_sp0", "m_sp1", "m_cacc")
            for j in range(4):
                dc = dcb * 4 + j
                sch.op("dve", lambda v, dc=dc, j=j: v.tensor_tensor(
                    out=m_v[:, j * TCH:(j + 1) * TCH],
                    in0=m_dt[:, dc * TCH:(dc + 1) * TCH],
                    in1=m_xc[:, dc * TCH:(dc + 1) * TCH], op=OP.mult),
                    reads=("m_dt", "m_xc"), writes=("m_v",))
                pass
            bc_fetch(0)
            bc_fetch(1)
            emit_dA(dcb, 0)
            for si in range(DS):
                bs = si % 2
                bcB = m_bc[:, bs * 2 * TCH:bs * 2 * TCH + TCH]
                bcC = m_bc[:, bs * 2 * TCH + TCH:bs * 2 * TCH + 2 * TCH]
                for j in range(4):
                    dbx = m_dbx[:, j * TCH:(j + 1) * TCH]
                    sch.op("dve", lambda v, j=j, dbx=dbx, bcB=bcB: v.tensor_tensor(
                        out=dbx, in0=m_v[:, j * TCH:(j + 1) * TCH],
                        in1=bcB, op=OP.mult),
                        reads=("m_v", f"m_bcB{bs}"), writes=(f"m_dbx{j}",))
                for j in range(4):
                    dc = dcb * 4 + j
                    dA = m_dA[:, j * TCH:(j + 1) * TCH]
                    dbx = m_dbx[:, j * TCH:(j + 1) * TCH]
                    hb = m_h[:, j * TCH:(j + 1) * TCH]
                    cini = m_carry[:, s * NDC * DS + dc * DS + si:
                                   s * NDC * DS + dc * DS + si + 1]
                    sch.op("dve", lambda v, dA=dA, dbx=dbx, hb=hb, cini=cini:
                           v.tensor_tensor_scan(out=hb, data0=dA, data1=dbx,
                                                initial=cini, op0=OP.mult,
                                                op1=OP.add),
                           reads=(f"m_dA{j}", f"m_dbx{j}", f"m_carry{j}"),
                           writes=(f"m_h{j}",))
                if si + 1 < DS:
                    emit_dA(dcb, si + 1)
                ypar = m_ye if si % 2 == 0 else m_yo
                ypk = "m_ye" if si % 2 == 0 else "m_yo"
                for j in range(4):
                    dc = dcb * 4 + j
                    dbx = m_dbx[:, j * TCH:(j + 1) * TCH]
                    hb = m_h[:, j * TCH:(j + 1) * TCH]
                    cini = m_carry[:, s * NDC * DS + dc * DS + si:
                                   s * NDC * DS + dc * DS + si + 1]
                    sch.op("dve", lambda v, hb=hb, cini=cini: v.tensor_copy(
                        out=cini, in_=hb[:, TCH - 1:TCH]),
                        reads=(f"m_h{j}",), writes=(f"m_carry{j}",))
                    sch.op("dve", lambda v, hb=hb, dbx=dbx, bcC=bcC: v.tensor_tensor(
                        out=dbx, in0=hb, in1=bcC, op=OP.mult),
                        reads=(f"m_h{j}", f"m_bcC{bs}"), writes=(f"m_dbx{j}",))
                for j in range(4):
                    dbx = m_dbx[:, j * TCH:(j + 1) * TCH]
                    if si < 2:
                        sch.op("dve", lambda v, j=j, dbx=dbx, ypar=ypar:
                               v.tensor_copy(
                                   out=ypar[:, j * TCH:(j + 1) * TCH], in_=dbx),
                               reads=(f"m_dbx{j}",), writes=(ypk,))
                    else:
                        sch.op("dve", lambda v, j=j, dbx=dbx, ypar=ypar:
                               v.tensor_tensor(
                                   out=ypar[:, j * TCH:(j + 1) * TCH],
                                   in0=ypar[:, j * TCH:(j + 1) * TCH],
                                   in1=dbx, op=OP.add),
                               reads=(f"m_dbx{j}", ypk), writes=(ypk,))
                if si + 2 < DS:
                    bc_fetch(si + 2)
            for j in range(4):
                dc = dcb * 4 + j
                sg = m_sg[:, dc * TCH:(dc + 1) * TCH]
                sch.op("dve", lambda v, j=j: v.tensor_tensor(
                    out=m_ye[:, j * TCH:(j + 1) * TCH],
                    in0=m_ye[:, j * TCH:(j + 1) * TCH],
                    in1=m_yo[:, j * TCH:(j + 1) * TCH], op=OP.add),
                    reads=("m_ye", "m_yo"), writes=("m_ye",))
                sch.op("dve", lambda v, j=j, dc=dc: v.scalar_tensor_tensor(
                    out=m_ye[:, j * TCH:(j + 1) * TCH],
                    in0=m_xc[:, dc * TCH:(dc + 1) * TCH],
                    scalar=cDp[:, dc:dc + 1],
                    in1=m_ye[:, j * TCH:(j + 1) * TCH], op0=OP.mult, op1=OP.add),
                    reads=("m_xc", "consts", "m_ye"), writes=("m_ye",))
                sch.op("dve", lambda v, j=j, sg=sg: v.tensor_tensor(
                    out=sg, in0=m_ye[:, j * TCH:(j + 1) * TCH], in1=sg, op=OP.mult),
                    reads=("m_ye", "m_sg"), writes=("m_sg",))

    def out_proj(s, ci):
        for k in range(NDC):
            wb = m_wk[:, (k % 4) * 1024:(k % 4) * 1024 + 1024]
            wk = f"m_wk{k % 4}"
            sch.dma(wb, scr["out_w"][k * 128:(k + 1) * 128, :], f"wm{k % 4}",
                    reads=("scr_out_w",), writes=(wk,))
            for tsub in range(4):
                for half in range(2):
                    bank = 2 * tsub + half
                    sch.op("pe", lambda T, k=k, tsub=tsub, half=half, bank=bank,
                           wb=wb: T.matmul(
                        out=pb[bank],
                        lhsT=m_sg[:, k * TCH + tsub * 128:k * TCH + tsub * 128 + 128],
                        rhs=wb[:, half * 512:half * 512 + 512],
                        start=(k == 0), stop=False),
                        reads=(wk, "m_sg"), writes=(f"ps{bank}",))
        for tsub in range(4):
            t = s * NTCH + ci * 4 + tsub
            for half in range(2):
                bank = 2 * tsub + half
                sch.op("pe", lambda T, half=half, bank=bank: T.matmul(
                    out=pb[bank], lhsT=ones_bf[0:1, :],
                    rhs=row("outb", half * 512, half * 512 + 512),
                    start=False, stop=True), reads=("consts",), writes=(f"ps{bank}",))
                sch.op("dve", lambda v, t=t, half=half, bank=bank: v.tensor_tensor(
                    out=res[:, t * D + half * 512:t * D + half * 512 + 512],
                    in0=pb[bank],
                    in1=res[:, t * D + half * 512:t * D + half * 512 + 512],
                    op=OP.add), reads=(f"ps{bank}", f"res{t}"), writes=(f"res{t}",))

    def mamba():
        names = ("m_xnT m_xz m_dt m_xc m_sg m_v m_ye m_yo m_bcB0 m_bcB1 m_bcC0 "
                 "m_bcC1 m_dA0 m_dA1 m_dA2 m_dA3 m_dAi0 m_dAi1 m_dAi2 m_dAi3 "
                 "m_sp0 m_sp1 m_sp2 m_cacc m_h0 m_h1 m_h2 m_h3 "
                 "m_dbx0 m_dbx1 m_dbx2 m_dbx3 m_carry0 m_carry1 m_carry2 m_carry3 m_bmcm m_bmT m_halo "
                 "m_mini m_wk0 m_wk1 m_wk2 m_wk3 m_bcw").split()
        for nm in names:
            sch.alias(nm, "f_xnT", "f_w1", "f_w2", "f_wg", "f_gk", "f_gh", "f_ghT")
        sch.op("dve", lambda v: v.memset(m_carry, 0.0),
               writes=("m_carry0", "m_carry1", "m_carry2", "m_carry3"))
        for ci in range(NMCH):
            for s in range(SEQ):
                mamba_chunk(s, ci)

    def maybe_tap(phase):
        if TAP[0] and TAP[0][0] == phase:
            nm = TAP[0][1]
            obj = {"xnT": (xnT, ("xnT",)), "QT": (QT, ("QT",)),
                   "KT": (KT, ("KT",)), "V65": (V65, ("V65",)),
                   "res": (res, tuple(f"res{t}" for t in range(32))),
                   "mhat": (mhat, ("mhat",)), "nrm": (nrm, ("nrm",)),
                   "m_xz": (m_xz, ("m_xz",)), "m_xc": (m_xc, ("m_xc",)),
                   "m_dt": (m_dt, ("m_dt",)), "m_sg": (m_sg, ("m_sg",)),
                   "f_g": (f_g, ("f_gk",)),
                   }[nm]
            dump(obj[0], obj[1])
            return True
        return False

    if n_sub >= 3:
        cast_weights()
    tapped = False
    intap = lambda: TAP[0] and TAP[0][0].endswith(("_0", "_1"))
    if n_sub >= 1:
        attention(0)
        tapped = maybe_tap("attn0") or (TAP[0] and TAP[0][0].endswith("_0"))
        if not tapped:
            attention(1)
            tapped = maybe_tap("attn1") or (TAP[0] and TAP[0][0].endswith("_1"))
    if n_sub >= 2 and not tapped:
        ffn(final=False)
        tapped = tapped or maybe_tap("ffn1") or (TAP[0] and TAP[0][0] == "gates")
    if n_sub >= 3 and not tapped:
        mamba()
        tapped = maybe_tap("mamba")
    if tapped:
        pass
    elif n_sub >= 4:
        ffn(final=True)
    else:
        for t in range(SEQ * NTCH):
            hh = t % 2
            sl = xstage[:, hh * 1024:(hh + 1) * 1024]
            sch.op("dve", lambda v, sl=sl, t=t: v.tensor_copy(
                out=sl, in_=res[:, t * D:(t + 1) * D]),
                reads=(f"res{t}",), writes=(f"os{hh}",))
            sch.dma(dout[t * 128:(t + 1) * 128, :], sl, "out", reads=(f"os{hh}",))
    global LAST_SCH
    LAST_SCH = sch
    sch.finalize()
    return nc


def kernel(**inputs):
    g = host_prepare(inputs)
    x = np.ascontiguousarray(np.asarray(inputs["L0_attn_a_x"], np.float32))
    nc = build_program(4)
    in_maps = []
    for core in range(NCORE):
        m = dict(g)
        m["x"] = np.ascontiguousarray(x[2 * core:2 * core + 2].reshape(SEQ * S, D))
        in_maps.append(m)
    r = run_bass_kernel_spmd(nc, in_maps, core_ids=list(range(NCORE)))
    return np.concatenate([rr["out"].reshape(SEQ, S, D) for rr in r.results], axis=0)
